# version 11
# baseline (speedup 1.0000x reference)
import numpy as np
import ml_dtypes
from contextlib import ExitStack

import concourse.bass as bass
import concourse.mybir as mybir
from concourse.bass_utils import run_bass_kernel_spmd

F32 = mybir.dt.float32
BF16 = mybir.dt.bfloat16
I8 = mybir.dt.int8
AF = mybir.ActivationFunctionType
ALU = mybir.AluOpType
AX = mybir.AxisListType

P = 128
D = 2048
NCD = 16
NH = 16
S = 4096
NSEG_K = 8
NQ = 2048
NSEG_Q = 4
NQI = 16
NKC = [qi // 2 + 1 for qi in range(NQI)]
DI = 8192
NJ = 64
ISQ = 0.08838834764831845
EPS = 1e-5

ENGINES = ("sync", "gpsimd", "scalar", "vector", "tensor")
DMAQ = {"dS": ("sync", 6), "dA": ("scalar", 4), "dG": ("gpsimd", 3)}
DMA_SEM_NAMES = [f"{q}{i}" for q, (_, n) in DMAQ.items() for i in range(n)]
ALL_SEMS = tuple(DMA_SEM_NAMES) + ("sT", "sV", "sA", "hV")
OWN = {"sync": {f"dS{i}" for i in range(6)},
       "gpsimd": {f"dG{i}" for i in range(3)},
       "scalar": {"sA"} | {f"dA{i}" for i in range(4)},
       "vector": {"sV"}, "tensor": {"sT"}}


def _flat(deps, out=None):
    if out is None:
        out = []
    for d in deps:
        if d is None:
            continue
        if isinstance(d, tuple) and len(d) == 2 and isinstance(d[0], str):
            out.append(d)
        else:
            _flat(d, out)
    return out


class Sched:
    def __init__(self):
        self.ops = {e: [] for e in ENGINES}
        self.cnt = {s: 0 for s in ALL_SEMS}
        self.maxwait = {s: 0 for s in ALL_SEMS}
        self.rot = {q: 0 for q in DMAQ}

    def do(self, eng, fn, deps=(), sig=None, hard=()):
        waits = {}
        for s, v in _flat(deps):
            if s in OWN[eng]:
                continue
            if waits.get(s, -1) < v:
                waits[s] = v
        for s, v in _flat(hard):
            if waits.get(s, -1) < v:
                waits[s] = v
        if sig in DMAQ:
            q = sig
            sig = f"{q}{self.rot[q] % DMAQ[q][1]}"
            self.rot[q] += 1
            prev = self.cnt[sig]
            if prev > 0 and waits.get(sig, -1) < prev:
                waits[sig] = prev
        for s, v in waits.items():
            if self.maxwait[s] < v:
                self.maxwait[s] = v
        tok = None
        inc = 0
        if sig is not None:
            inc = 16 if sig in DMA_SEM_NAMES else 1
            self.cnt[sig] += inc
            tok = (sig, self.cnt[sig])
        self.ops[eng].append((sorted(waits.items()), fn, sig, inc))
        return tok

    def hz(self, eng, fn, deps=()):
        assert eng == "vector"
        tok = self.do(eng, fn, deps, sig="hV")
        self.ops[eng].append(([("hV", tok[1])], None, None, 0))
        self.maxwait["hV"] = max(self.maxwait["hV"], tok[1])
        return tok

    def bar(self):
        return [(s, self.cnt[s]) for s in ALL_SEMS]

    def check(self):
        for s in ALL_SEMS:
            assert self.maxwait[s] <= self.cnt[s], (s, self.maxwait[s], self.cnt[s])


def build_program(dbg=False):
    nc = bass.Bass("TRN2", target_bir_lowering=False)
    SCR = "ExternalOutput" if dbg else "Internal"

    dx = nc.dram_tensor("x", [S, D], F32, kind="ExternalInput").ap()
    dxq = nc.dram_tensor("xq", [NQ, D], F32, kind="ExternalInput").ap()
    dxqt = nc.dram_tensor("xqt", [NCD, P, NQ], F32, kind="ExternalInput").ap()
    dqm = nc.dram_tensor("qmask", [P, NQI * 512], BF16, kind="ExternalInput").ap()
    dqmt = nc.dram_tensor("qmt", [P, 256], BF16, kind="ExternalInput").ap()
    dwq = nc.dram_tensor("wq", [NCD, P, D], F32, kind="ExternalInput").ap()
    dwk = nc.dram_tensor("wk", [NCD, P, D], F32, kind="ExternalInput").ap()
    dwv = nc.dram_tensor("wv", [NCD, P, D], F32, kind="ExternalInput").ap()
    dwo = nc.dram_tensor("wo", [NCD, P, D], F32, kind="ExternalInput").ap()
    dbq = nc.dram_tensor("bq", [NCD, P], F32, kind="ExternalInput").ap()
    dbk = nc.dram_tensor("bk", [NCD, P], F32, kind="ExternalInput").ap()
    dbv = nc.dram_tensor("bv", [NCD, P], F32, kind="ExternalInput").ap()
    dbo = nc.dram_tensor("bo", [NCD, P], F32, kind="ExternalInput").ap()
    dwg = nc.dram_tensor("wg8", [NJ, P, NCD, P], I8, kind="ExternalInput").ap()
    dwu = nc.dram_tensor("wu8", [NJ, P, NCD, P], I8, kind="ExternalInput").ap()
    dwd = nc.dram_tensor("wd8", [NCD, 8, P, 1024], I8, kind="ExternalInput").ap()
    dsg = nc.dram_tensor("sg", [NJ, P], F32, kind="ExternalInput").ap()
    dsu = nc.dram_tensor("su", [NJ, P], F32, kind="ExternalInput").ap()
    dsd = nc.dram_tensor("sd", [NCD, P], F32, kind="ExternalInput").ap()
    dout = nc.dram_tensor("out", [NCD, P, NQ], F32, kind="ExternalOutput").ap()

    xnscr = nc.dram_tensor("xnscr", [NCD, P, S], BF16, kind=SCR).ap()
    xnqscr = nc.dram_tensor("xnqscr", [NCD, P, NQ], BF16, kind=SCR).ap()
    kscr = nc.dram_tensor("kscr", [NH, P, S], BF16, kind=SCR).ap()
    vscr = nc.dram_tensor("vscr", [S, D], BF16, kind=SCR).ap()
    qscr = nc.dram_tensor("qscr", [NH, P, NQ], BF16, kind=SCR).ap()
    sscr = nc.dram_tensor("sscr", [NQI, P, D], BF16, kind=SCR).ap()
    fusescr = nc.dram_tensor("fusescr", [NSEG_Q, P, NJ * 512], BF16, kind=SCR).ap()
    dwnscr = nc.dram_tensor("dwnscr", [NSEG_Q, 2, 2, 4, P, 512], F32, kind=SCR).ap()
    wgbf = nc.dram_tensor("wgbf", [NJ, P, D], BF16, kind="Internal").ap()
    wubf = nc.dram_tensor("wubf", [NJ, P, D], BF16, kind="Internal").ap()
    wdbf = nc.dram_tensor("wdbf", [NCD, 8, P, 1024], BF16, kind="Internal").ap()

    with ExitStack() as st:
        en = st.enter_context
        idb = en(nc.sbuf_tensor("idb", [P, P], BF16))
        id32 = en(nc.sbuf_tensor("id32", [P, P], F32))
        qmb = en(nc.sbuf_tensor("qmb", [P, NQI * 512], BF16))
        konst = en(nc.sbuf_tensor("konst", [P, 4], F32))
        bqs = en(nc.sbuf_tensor("bqs", [P, NCD], F32))
        bks = en(nc.sbuf_tensor("bks", [P, NCD], F32))
        bvs = en(nc.sbuf_tensor("bvs", [P, NCD], F32))
        sgc = en(nc.sbuf_tensor("sgc", [P, NJ], F32))
        suc = en(nc.sbuf_tensor("suc", [P, NJ], F32))
        boc = en(nc.sbuf_tensor("boc", [P, NCD], F32))
        sdc = en(nc.sbuf_tensor("sdc", [P, NCD], F32))
        xf = en(nc.sbuf_tensor("xf", [P, 2, 2, D], F32))
        outb = en(nc.sbuf_tensor("outb", [P, 2, 512], F32))
        tmpg = en(nc.sbuf_tensor("tmpg", [P, 2, 512], F32))
        stats = en(nc.sbuf_tensor("stats", [P, 64], F32))
        lnbuf = en(nc.sbuf_tensor("lnbuf", [P, 8192], BF16))
        arA = en(nc.sbuf_tensor("arA", [P, 32768], BF16))
        arC = en(nc.sbuf_tensor("arC", [P, 16384], BF16))
        qhb = en(nc.sbuf_tensor("qhb", [P, 4096], BF16))
        pstg = en(nc.sbuf_tensor("pstg", [P, 2, 1024], BF16))
        qmtb = en(nc.sbuf_tensor("qmtb", [P, 256], BF16))
        onesb = en(nc.sbuf_tensor("onesb", [1, P], BF16))
        onek = en(nc.sbuf_tensor("onek", [P, 1], BF16))
        nmrow = en(nc.sbuf_tensor("nmrow", [1, 2, 512], BF16))
        pon = en(nc.sbuf_tensor("pon", [P, 2, P], BF16))
        gbuf = en(nc.sbuf_tensor("gbuf", [P, 2, 512], BF16))

        psA = en(nc.psum_tensor("psA", [P, 1024], F32))
        psE = en(nc.psum_tensor("psE", [P, 1024], F32))
        psO0 = en(nc.psum_tensor("psO0", [P, 512], F32))
        psO1 = en(nc.psum_tensor("psO1", [P, 512], F32))
        psB = en(nc.psum_tensor("psB", [P, 1024], BF16))
        psB2 = en(nc.psum_tensor("psB2", [P, 1024], BF16))
        ps = [psA[:, 0:512], psA[:, 512:1024], psE[:, 0:512], psE[:, 512:1024],
              psO0[:, 0:512], psO1[:, 0:512]]
        psOP = [psO0, psO1]

        vr_bo = xf[0:16, 1, 0, 0:128]
        vr_sd = xf[0:16, 1, 0, 128:256]
        vr_bq = xf[0:16, 1, 1, 0:128]
        vr_bk = xf[0:16, 1, 1, 128:256]
        vr_bv = xf[0:16, 1, 1, 256:384]
        vr_sg = xf[0:64, 0, 0, 0:128]
        vr_su = xf[0:64, 0, 0, 128:256]
        sqscr = qhb[:, 0:2048]

        SEM = {s: en(nc.semaphore(s)) for s in ALL_SEMS}
        sc = Sched()

        TRB = [(psB, psB[:, 0:512]), (psB2, psB2[:, 0:512])]

        sc.do("sync", lambda e: e.dma_start(out=qmb[:], in_=dqm[:, :]), (), "dS")
        sc.do("sync", lambda e: e.dma_start(out=qmtb[:], in_=dqmt[:, :]), (), "dS")
        sc.do("sync", lambda e: e.dma_start(out=vr_bq, in_=dbq[:, :]), (), "dS")
        sc.do("sync", lambda e: e.dma_start(out=vr_bk, in_=dbk[:, :]), (), "dS")
        sc.do("sync", lambda e: e.dma_start(out=vr_bv, in_=dbv[:, :]), (), "dS")
        sc.do("sync", lambda e: e.dma_start(out=vr_sg, in_=dsg[:, :]), (), "dS")
        sc.do("sync", lambda e: e.dma_start(out=vr_su, in_=dsu[:, :]), (), "dS")
        sc.do("sync", lambda e: e.dma_start(out=vr_bo, in_=dbo[:, :]), (), "dS")
        t_ld0 = sc.do("sync", lambda e: e.dma_start(out=vr_sd, in_=dsd[:, :]), (), "dS")

        def _setup_g(e):
            e.memset(idb[:], 1.0)
            e.affine_select(idb[:], idb[:], [[-1, P]], ALU.is_equal, 0.0,
                            base=0, channel_multiplier=1)
            e.memset(id32[:], 1.0)
            e.affine_select(id32[:], id32[:], [[-1, P]], ALU.is_equal, 0.0,
                            base=0, channel_multiplier=1)
            e.memset(onesb[:], 1.0)
            e.memset(onek[:], 1.0)
            return e.memset(konst[:, 0:1], EPS)
        t_g0 = sc.do("gpsimd", _setup_g, (), "sV")

        for (src, dst, fl) in ((dwg, wgbf, True), (dwu, wubf, True), (dwd, wdbf, False)):
            for q4 in range(4):
                a, b_ = q4 * 4, (q4 + 1) * 4
                if fl:
                    a, b_ = q4 * (NJ // 4), (q4 + 1) * (NJ // 4)
                    fn = (lambda e, s_=src, d_=dst, a=a, b=b_:
                          e.dma_start(out=d_[a:b], in_=s_[a:b].rearrange("a p c m -> a p (c m)")))
                else:
                    fn = (lambda e, s_=src, d_=dst, a=a, b=b_:
                          e.dma_start(out=d_[a:b], in_=s_[a:b].rearrange("a g p m -> a g p m")))
                sc.do("gpsimd", fn, (), "dG")

        t_tr = sc.do("tensor", lambda e: e.transpose(ps[0][:, 0:16], vr_bq, id32[0:16, 0:16]), (t_ld0, t_g0), "sT")
        t_tr = sc.do("tensor", lambda e: e.transpose(ps[0][:, 16:32], vr_bk, id32[0:16, 0:16]), (), "sT")
        t_tr = sc.do("tensor", lambda e: e.transpose(ps[0][:, 32:48], vr_bv, id32[0:16, 0:16]), (), "sT")
        t_tr = sc.do("tensor", lambda e: e.transpose(ps[0][:, 64:128], vr_sg, id32[0:64, 0:64]), (), "sT")
        t_tr = sc.do("tensor", lambda e: e.transpose(ps[0][:, 128:192], vr_su, id32[0:64, 0:64]), (), "sT")
        t_tr = sc.do("tensor", lambda e: e.transpose(ps[0][:, 192:208], vr_bo, id32[0:16, 0:16]), (), "sT")
        t_tr = sc.do("tensor", lambda e: e.transpose(ps[0][:, 208:224], vr_sd, id32[0:16, 0:16]), (), "sT")
        sc.do("scalar", lambda e: e.activation(out=bqs[:, :], in_=ps[0][:, 0:16], func=AF.Copy, scale=ISQ), (t_tr,), "sA")
        sc.do("scalar", lambda e: e.copy(bks[:, :], ps[0][:, 16:32]), (), "sA")
        sc.do("scalar", lambda e: e.copy(bvs[:, :], ps[0][:, 32:48]), (), "sA")
        sc.do("scalar", lambda e: e.copy(sgc[:, :], ps[0][:, 64:128]), (), "sA")
        sc.do("scalar", lambda e: e.copy(suc[:, :], ps[0][:, 128:192]), (), "sA")
        sc.do("scalar", lambda e: e.copy(boc[:, :], ps[0][:, 192:208]), (), "sA")
        sc.do("scalar", lambda e: e.copy(sdc[:, :], ps[0][:, 208:224]), (), "sA")

        BAR0 = sc.bar()

        xnstage = [arA[:, 0:8192].rearrange("p (c t) -> p c t", t=512),
                   arA[:, 8192:16384].rearrange("p (c t) -> p c t", t=512)]
        xntm = [lnbuf[:, 0:2048], lnbuf[:, 2048:4096]]
        grp_i = [0]

        def emit_ln(src_ap, dst_scr, ntok, bar):
            nhs = ntok // 256
            xf_free = [None, None]
            stage_free = [None, None]
            trb_free = [None, None]
            hist = {}
            store_toks = []
            for hsg in range(nhs):
                sl = hsg % 2
                seg = hsg // 2
                ssl = seg % 2
                tld = sc.do("sync",
                            lambda e, sl=sl, o=hsg * 256, src_ap=src_ap:
                            e.dma_start(out=xf[:, sl, :, :],
                                        in_=src_ap[o:o + 256, :].rearrange("(n p) d -> p n d", p=P)),
                            [xf_free[sl]] + (bar if hsg < 2 else []), "dS")
                xf_readers = []
                for i in range(2):
                    tt = hsg * 2 + i
                    c0 = (tt % 4) * 8
                    xin = xf[:, sl, i, :]
                    old = hist.get(tt - 4, {})
                    tsq = sc.do("scalar",
                                lambda e, xin=xin, c0=c0:
                                e.activation(out=sqscr[:, :], in_=xin, func=AF.Square,
                                             accum_out=stats[:, c0:c0 + 1]),
                                (tld, old.get("var")) + (tuple(bar) if tt < 2 else ()), "sA")
                    tns = sc.do("vector",
                                lambda e, xin=xin, c0=c0:
                                e.reduce_sum(stats[:, c0 + 1:c0 + 2], xin, axis=AX.X, negate=True),
                                (tld, old.get("nm"), old.get("ms")), "sV")
                    tnm = sc.do("scalar",
                                lambda e, c0=c0:
                                e.activation(out=stats[:, c0 + 2:c0 + 3], in_=stats[:, c0 + 1:c0 + 2],
                                             func=AF.Copy, scale=1.0 / D),
                                (tns, old.get("xn")), "sA")
                    tms = sc.do("scalar",
                                lambda e, c0=c0:
                                e.activation(out=stats[:, c0 + 3:c0 + 4], in_=stats[:, c0 + 1:c0 + 2],
                                             func=AF.Square, scale=1.0 / D),
                                (tns, old.get("var")), "sA")
                    tvar = sc.do("vector",
                                 lambda e, c0=c0:
                                 e.scalar_tensor_tensor(out=stats[:, c0 + 4:c0 + 5],
                                                        in0=stats[:, c0:c0 + 1], scalar=1.0 / D,
                                                        in1=stats[:, c0 + 3:c0 + 4],
                                                        op0=ALU.mult, op1=ALU.subtract),
                                 (tsq, tms, old.get("std")), "sV")
                    tstd = sc.do("scalar",
                                 lambda e, c0=c0:
                                 e.activation(out=stats[:, c0 + 5:c0 + 6], in_=stats[:, c0 + 4:c0 + 5],
                                              func=AF.Sqrt, bias=konst[:, 0:1]),
                                 (tvar, old.get("ri")), "sA")
                    tri = sc.hz("vector",
                                lambda e, c0=c0:
                                e.reciprocal(stats[:, c0 + 6:c0 + 7], stats[:, c0 + 5:c0 + 6]),
                                (tstd,))
                    xsl = tt % 2
                    txn = sc.do("vector",
                                lambda e, xin=xin, c0=c0, xsl=xsl:
                                e.tensor_scalar(out=xntm[xsl][:, :], in0=xin,
                                                scalar1=stats[:, c0 + 2:c0 + 3],
                                                scalar2=stats[:, c0 + 6:c0 + 7],
                                                op0=ALU.add, op1=ALU.mult),
                                (tri, tnm, hist.get(tt - 2, {}).get("tr_last")), "sV")
                    xf_readers += [tsq, tns, txn]
                    evs = []
                    trl = None
                    for g in range(4):
                        gsl = grp_i[0] % 2
                        grp_i[0] += 1
                        bank, bslice = TRB[gsl]

                        def trfn(e, xsl=xsl, g=g, bslice=bslice):
                            for cc in range(4):
                                c = g * 4 + cc
                                i_ = e.transpose(bslice[:, cc * 128:(cc + 1) * 128],
                                                 xntm[xsl][:, c * 128:(c + 1) * 128], idb[:])
                            return i_
                        ttr = sc.do("tensor", trfn, (txn, trb_free[gsl], t_g0), "sT")
                        trl = ttr
                        tev = sc.do("vector",
                                    lambda e, ssl=ssl, tt=tt, g=g, bslice=bslice:
                                    e.tensor_copy(xnstage[ssl][:, g * 4:(g + 1) * 4, (tt % 4) * 128:(tt % 4 + 1) * 128],
                                                  bslice.rearrange("p (c t) -> p c t", t=P)),
                                    (ttr, stage_free[ssl]), "sV")
                        trb_free[gsl] = tev
                        evs.append(tev)
                    hist[tt] = {"ns": tns, "nm": tnm, "ms": tms, "var": tvar,
                                "std": tstd, "ri": tri, "xn": txn,
                                "tr_last": trl, "evs": evs}
                xf_free[sl] = xf_readers
                if hsg % 2 == 1:
                    all_evs = (hist[hsg * 2 - 2]["evs"] + hist[hsg * 2 - 1]["evs"] +
                               hist[hsg * 2]["evs"] + hist[hsg * 2 + 1]["evs"])
                    tst = sc.do("scalar",
                                lambda e, seg=seg, ssl=ssl, dst_scr=dst_scr:
                                e.dma_start(out=dst_scr[:, :, seg * 512:(seg + 1) * 512].rearrange("c p t -> p c t"),
                                            in_=xnstage[ssl][:, :, :]),
                                all_evs, "dA")
                    stage_free[ssl] = tst
                    store_toks.append(tst)
            return store_toks

        emit_ln(dx, xnscr, S, BAR0)
        emit_ln(dxq, xnqscr, NQ, sc.bar())
        BAR1a = sc.bar()

        wres = arC[:, 0:16384].rearrange("p (c m) -> p c m", m=1024)
        xnb = [arA[:, 16384:24576].rearrange("p (c t) -> p c t", t=512),
               arA[:, 24576:32768].rearrange("p (c t) -> p c t", t=512)]
        kstage = [arA[:, 0:4096].rearrange("p (h t) -> p h t", t=512),
                  arA[:, 4096:8192].rearrange("p (h t) -> p h t", t=512)]
        vstage = [arA[:, 0:1024], arA[:, 1024:2048]]

        def emit_kq(wsrc, biascol, scale, src_scr, ntokens, dst_scr, bar):
            nseg = ntokens // 512
            wres_free = bar
            xnb_free = [None, None]
            bank_free = [None, None]
            kst_free = [None, None]
            last_toks = []
            for p_ in range(2):
                tw = sc.do("gpsimd",
                           lambda e, p_=p_, wsrc=wsrc:
                           e.dma_start(out=wres[:, :, :],
                                       in_=wsrc[:, :, p_ * 1024:(p_ + 1) * 1024].rearrange("c p m -> p c m")),
                           [wres_free] + (bar if p_ == 0 else []), "dG")
                for seg in range(nseg):
                    sl = seg % 2
                    tx = sc.do("sync",
                               lambda e, seg=seg, sl=sl, src_scr=src_scr:
                               e.dma_start(out=xnb[sl][:, :, :],
                                           in_=src_scr[:, :, seg * 512:(seg + 1) * 512].rearrange("c p t -> p c t")),
                               [xnb_free[sl]] + (bar if p_ == 0 and seg < 2 else []), "dS")
                    last_ev = None
                    for hh in range(8):
                        bsl = hh % 2
                        bank = ps[bsl]

                        def mmfn(e, sl=sl, hh=hh, bank=bank):
                            for c in range(NCD):
                                i_ = e.matmul(bank[:, 0:512],
                                              wres[:, c, hh * 128:(hh + 1) * 128],
                                              xnb[sl][:, c, :],
                                              start=(c == 0), stop=(c == NCD - 1))
                            return i_
                        tmm = sc.do("tensor", mmfn, (tw, tx, bank_free[bsl]), "sT")
                        gh = p_ * 8 + hh
                        tev = sc.do("scalar",
                                    lambda e, sl=sl, hh=hh, bank=bank, gh=gh, biascol=biascol, scale=scale:
                                    e.activation(out=kstage[sl][:, hh, :], in_=bank[:, 0:512],
                                                 func=AF.Identity, bias=biascol[:, gh:gh + 1], scale=scale),
                                    (tmm, kst_free[sl]), "sA")
                        bank_free[bsl] = tev
                        last_ev = tev
                    tst = sc.do("scalar",
                                lambda e, p_=p_, seg=seg, sl=sl, dst_scr=dst_scr:
                                e.dma_start(out=dst_scr[p_ * 8:(p_ + 1) * 8, :, seg * 512:(seg + 1) * 512].rearrange("h p t -> p h t"),
                                            in_=kstage[sl][:, :, :]),
                                (), "dA", hard=(last_ev,))
                    kst_free[sl] = tst
                    xnb_free[sl] = tst
                    last_toks.append(tst)
                wres_free = last_toks[-1]
            return last_toks

        def emit_v(bar):
            wres_free = bar
            xnb_free = [None, None]
            bank_free = [None, None]
            vst_free = [None, None]
            last_toks = []
            for p_ in range(2):
                tw = sc.do("gpsimd",
                           lambda e, p_=p_:
                           e.dma_start(out=wres[:, :, :],
                                       in_=dwv[:, :, p_ * 1024:(p_ + 1) * 1024].rearrange("c p m -> p c m")),
                           [wres_free] + (bar if p_ == 0 else []), "dG")
                for seg in range(NSEG_K):
                    sl = seg % 2
                    tx = sc.do("sync",
                               lambda e, seg=seg, sl=sl:
                               e.dma_start(out=xnb[sl][:, :, :],
                                           in_=xnscr[:, :, seg * 512:(seg + 1) * 512].rearrange("c p t -> p c t")),
                               [xnb_free[sl]] + (bar if p_ == 0 and seg < 2 else []), "dS")
                    seg_toks = []
                    for sub in range(4):
                        vsl = sub % 2
                        evs = []
                        for dt in range(2):
                            bank = ps[2 + dt]

                            def mmfn(e, sl=sl, sub=sub, dt=dt, bank=bank):
                                for c in range(NCD):
                                    i_ = e.matmul(bank[:, 0:512],
                                                  xnb[sl][:, c, sub * 128:(sub + 1) * 128],
                                                  wres[:, c, dt * 512:(dt + 1) * 512],
                                                  start=(c == 0), stop=(c == NCD - 1))
                                return i_
                            tmm = sc.do("tensor", mmfn, (tw, tx, bank_free[dt]), "sT")
                            tev = sc.do("vector",
                                        lambda e, vsl=vsl, dt=dt, bank=bank:
                                        e.tensor_copy(vstage[vsl][:, dt * 512:(dt + 1) * 512], bank[:, 0:512]),
                                        (tmm, vst_free[vsl]), "sV")
                            bank_free[dt] = tev
                            evs.append(tev)
                        tstv = sc.do("scalar",
                                     lambda e, p_=p_, seg=seg, sub=sub, vsl=vsl:
                                     e.dma_start(out=vscr[seg * 512 + sub * 128: seg * 512 + (sub + 1) * 128,
                                                          p_ * 1024:(p_ + 1) * 1024],
                                                 in_=vstage[vsl][:, :]),
                                     evs, "dA")
                        vst_free[vsl] = tstv
                        seg_toks.append(tstv)
                    xnb_free[sl] = seg_toks
                    last_toks += seg_toks
                wres_free = last_toks[-1]
            return last_toks

        emit_kq(dwk, bks, 1.0, xnscr, S, kscr, BAR1a)
        emit_v(sc.bar())
        emit_kq(dwq, bqs, ISQ, xnqscr, NQ, qscr, sc.bar())
        BAR1 = sc.bar()

        kbuf = [arC[:, 0:4096], arC[:, 4096:8192]]
        vbuf = [arC[:, 8192:12288].rearrange("p (n d) -> p n d", d=P),
                arC[:, 12288:16384].rearrange("p (n d) -> p n d", d=P)]
        qbuf = [qhb[:, 0:2048], qhb[:, 2048:4096]]
        attn = arA
        NKS = [2 * qi + 2 for qi in range(NQI)]
        NHM = [(NKS[qi] + 3) // 4 for qi in range(NQI)]

        kq_free = [None, None]
        v_free = [None, None]
        psA_free = [None, None]
        psE_free = [None, None]
        pstf2 = pstg.rearrange("p a t -> p (a t)")
        pstg4 = [pstf2[:, i * 512:(i + 1) * 512] for i in range(4)]
        pst_free = [None, None, None, None]
        psO_free = [None, None]
        trb_free = [None, None]
        pon_free = [None, None]
        nmcol_free = [None, None]
        nmrow_free = [None, None]
        nmrow_tok = [None, None]
        gm = [0]
        pm = [0]

        def p2cols(par):
            b = 32 * par
            return dict(cm=b, nm=b + 16)

        for h in range(NH):
            hl = h % 2
            dep_ld = [kq_free[hl]] + (BAR1 if h < 2 else [])
            tk = sc.do("sync",
                       lambda e, h=h, hl=hl: e.dma_start(out=kbuf[hl][:, :], in_=kscr[h]),
                       dep_ld, "dS")
            tq = sc.do("sync",
                       lambda e, h=h, hl=hl: e.dma_start(out=qbuf[hl][:, :], in_=qscr[h]),
                       dep_ld, "dS")
            tvl = sc.do("sync",
                        lambda e, h=h, hl=hl:
                        e.dma_start(out=vbuf[hl][:, :, :],
                                    in_=vscr.rearrange("(n p) d -> p n d", p=P)[:, :, h * 128:(h + 1) * 128]),
                        [v_free[hl]] + (BAR1 if h < 2 else []), "dS")
            kq_readers = []
            v_readers = []
            st1 = {}
            pend = [None]

            def p1_chunk(qi, kc):
                par = qi % 2
                cols = p2cols(par)
                nkc = NKC[qi]
                bsl = kc % 2
                bank = psA[:, bsl * 512:(bsl + 1) * 512]
                last = (kc == nkc - 1)
                tmm = sc.do("tensor",
                            lambda e, qi=qi, kc=kc, hl=hl, bank=bank:
                            e.matmul(bank, qbuf[hl][:, qi * 128:(qi + 1) * 128],
                                     kbuf[hl][:, kc * 512:(kc + 1) * 512],
                                     start=True, stop=True),
                            (tk, tq, psA_free[bsl]), "sT")
                kq_readers.append(tmm)
                dep = tmm
                if last:
                    dep = sc.hz("vector",
                                lambda e, qi=qi, bank=bank:
                                e.tensor_add(bank, bank, qmb[:, qi * 512:(qi + 1) * 512]),
                                (tmm,))
                    tred = sc.hz("vector",
                                 lambda e, kc=kc, bank=bank, cc=cols["cm"]:
                                 e.reduce_max(stats[:, cc + kc:cc + kc + 1], bank, axis=AX.X),
                                 (dep,))
                else:
                    tred = sc.do("vector",
                                 lambda e, kc=kc, bank=bank, cc=cols["cm"]:
                                 e.reduce_max(stats[:, cc + kc:cc + kc + 1], bank, axis=AX.X),
                                 (dep,), "sV")
                psA_free[bsl] = tred
                st1.setdefault(qi, []).append(tred)
                return tred

            def p1_tail(qi):
                par = qi % 2
                cols = p2cols(par)
                tnm = sc.do("vector",
                            lambda e, nkc=NKC[qi], cc=cols["cm"], cn=cols["nm"]:
                            e.reduce_max(stats[:, cn:cn + 1], stats[:, cc:cc + nkc],
                                         axis=AX.X, negate=True),
                            (st1[qi][-1], nmcol_free[par]), "sV")
                ttr = sc.do("tensor",
                            lambda e, par=par, cn=cols["nm"]:
                            e.transpose(psOP[par][0:1, 256:384],
                                        stats[:, cn:cn + 1], id32[:]),
                            (tnm, psO_free[par]), "sT")
                nmcol_free[par] = ttr
                for rep in range(4):
                    tev = sc.do("scalar",
                                lambda e, par=par, rep=rep:
                                e.copy(nmrow[0:1, par, rep * P:(rep + 1) * P],
                                       psOP[par][0:1, 256:384]),
                                (ttr, nmrow_free[par]) if rep == 0 else (), "sA")
                nmrow_tok[par] = tev
                st1[qi] = tev

            def p2_tail(qj, pv_last):
                par = qj % 2
                cols = p2cols(par)
                tli = sc.hz("vector",
                            lambda e, par=par, cn=cols["nm"]:
                            e.reciprocal(stats[:, cn + 1:cn + 2], psOP[par][:, 128:129]),
                            (pv_last,))
                tno = sc.do("vector",
                            lambda e, par=par, cn=cols["nm"]:
                            e.tensor_scalar_mul(pon[:, par, :], psOP[par][:, 0:128],
                                                stats[:, cn + 1:cn + 2]),
                            (tli, pon_free[par]), "sV")
                psO_free[par] = tno
                gsl = gm[0] % 2
                gm[0] += 1
                bankT, bslice = TRB[gsl]
                tot = sc.do("tensor",
                            lambda e, par=par, bslice=bslice:
                            e.transpose(bslice[:, 0:128], pon[:, par, :], idb[:]),
                            (tno, trb_free[gsl]), "sT")
                tfin = sc.do("scalar",
                             lambda e, h=h, qj=qj, bslice=bslice:
                             e.activation(out=attn[:, h * 2048 + qj * 128: h * 2048 + (qj + 1) * 128],
                                          in_=bslice[:, 0:128],
                                          func=AF.Identity, bias=bvs[:, h:h + 1]),
                             (tot,), "sA")
                trb_free[gsl] = tfin
                pon_free[par] = tot

            def flush_pend():
                if pend[0] is None:
                    return
                qj, hm, nks, psl, texp = pend[0]
                pend[0] = None
                par = qj % 2
                last = (hm == NHM[qj] - 1)

                def pvfn(e, qj=qj, hm=hm, nks=nks, psl=psl, hl=hl, par=par, last=last):
                    for ks in range(nks):
                        e.matmul(psOP[par][:, 0:128],
                                 pstg4[psl][:, ks * 128:(ks + 1) * 128],
                                 vbuf[hl][:, hm * 4 + ks, :],
                                 start=(hm == 0 and ks == 0), stop=False)
                        i_ = e.matmul(psOP[par][:, 128:129],
                                      pstg4[psl][:, ks * 128:(ks + 1) * 128],
                                      onek[:, 0:1],
                                      start=False, stop=(last and ks == nks - 1))
                    return i_
                tpv = sc.do("tensor", pvfn,
                            (texp, tvl) + ((psO_free[par],) if hm == 0 else ()), "sT")
                pst_free[psl] = tpv
                v_readers.append(tpv)
                if last:
                    p2_tail(qj, tpv)

            def p2_half(qj, hm):
                par = qj % 2
                nks_tot = NKS[qj]
                nks = min(4, nks_tot - 4 * hm)
                last = (hm == NHM[qj] - 1)
                esl = hm % 2
                bank = psE[:, esl * 512:(esl + 1) * 512]

                def mmfn(e, qj=qj, hm=hm, nks=nks, hl=hl, par=par, bank=bank):
                    for ks in range(nks):
                        ksg = hm * 4 + ks
                        e.matmul(bank[:, ks * 128:(ks + 1) * 128],
                                 kbuf[hl][:, ksg * 128:(ksg + 1) * 128],
                                 qbuf[hl][:, qj * 128:(qj + 1) * 128],
                                 start=(ks == 0), stop=False)
                    i_ = e.matmul(bank[:, 0:nks * 128],
                                  onesb[0:1, 0:P],
                                  nmrow[0:1, par, 0:nks * 128],
                                  start=False, stop=True)
                    return i_
                tmm = sc.do("tensor", mmfn, (psE_free[esl], nmrow_tok[par], tk, tq), "sT")
                kq_readers.append(tmm)
                dep = tmm
                if last:
                    off2 = nks - 2
                    dep = sc.do("vector",
                                lambda e, off2=off2, bank=bank:
                                e.tensor_add(bank[:, off2 * 128:(off2 + 2) * 128],
                                             bank[:, off2 * 128:(off2 + 2) * 128],
                                             qmtb[:, :]),
                                (tmm,), "sV")
                psl = pm[0] % 4
                pm[0] += 1
                texp = sc.do("scalar",
                             lambda e, nks=nks, psl=psl, bank=bank:
                             e.activation(out=pstg4[psl][:, 0:nks * 128],
                                          in_=bank[:, 0:nks * 128], func=AF.Exp),
                             (dep, pst_free[psl]), "sA")
                psE_free[esl] = texp
                flush_pend()
                pend[0] = (qj, hm, nks, psl, texp)
                nmrow_free[par] = tmm
            for qi in range(NQI + 1):
                nA = NKC[qi] if qi < NQI else 0
                nB = NHM[qi - 1] if qi >= 1 else 0
                for i in range(max(nA, nB)):
                    if i < nA:
                        p1_chunk(qi, i)
                    if i < nB:
                        p2_half(qi - 1, i)
                if qi < NQI:
                    p1_tail(qi)
            flush_pend()
            kq_free[hl] = kq_readers
            v_free[hl] = v_readers
        BAR2 = sc.bar()

        sstg2 = [pstg.rearrange("p a t -> p (a t)")[:, i * 512:(i + 1) * 512] for i in range(4)]
        wres_free = BAR2
        bank_free = [None, None]
        sst_free = [None, None, None, None]
        s_toks = {}
        for p_ in range(2):
            tw = sc.do("gpsimd",
                       lambda e, p_=p_:
                       e.dma_start(out=wres[:, :, :],
                                   in_=dwo[:, :, p_ * 1024:(p_ + 1) * 1024].rearrange("c p m -> p c m")),
                       [wres_free] + (BAR2 if p_ == 0 else []), "dG")
            for g_ in range(4):
                for obl in range(8):
                    obg = p_ * 8 + obl
                    bsl = obl % 2
                    bank = ps[bsl]
                    ssl = (g_ * 8 + obl) % 4

                    def mmfn(e, g_=g_, obl=obl, bank=bank):
                        for c in range(NCD):
                            i_ = e.matmul(bank,
                                          wres[:, c, obl * 128:(obl + 1) * 128],
                                          attn[:, c * 2048 + g_ * 512: c * 2048 + (g_ + 1) * 512],
                                          start=(c == 0), stop=(c == NCD - 1))
                        return i_
                    tmm = sc.do("tensor", mmfn,
                                [tw, bank_free[bsl]] + (BAR2 if p_ == 0 and g_ == 0 and obl == 0 else []), "sT")
                    tev = sc.do("scalar",
                                lambda e, obg=obg, ssl=ssl, bank=bank:
                                e.activation(out=sstg2[ssl][:, :], in_=bank,
                                             func=AF.Identity, bias=boc[:, obg:obg + 1]),
                                (tmm, sst_free[ssl]), "sA")
                    bank_free[bsl] = tev
                    tst = sc.do("scalar",
                                lambda e, obg=obg, g_=g_, ssl=ssl:
                                e.dma_start(out=sscr[obg, :, g_ * 512:(g_ + 1) * 512],
                                            in_=sstg2[ssl][:, :]),
                                (), "dA", hard=(tev,))
                    sst_free[ssl] = tst
                    s_toks[(obg, g_)] = tst
            wres_free = tst
        BAR2B = sc.bar()

        fuse = arA.rearrange("p (j t) -> p j t", t=512)
        xnseg = lnbuf.rearrange("p (c t) -> p c t", t=512)
        sbufs = [arC[:, 0:512], arC[:, 512:1024], arC[:, 1024:1536], arC[:, 1536:2048]]
        wgr = [arC[:, 4096 + k * 2048: 4096 + (k + 1) * 2048] for k in range(3)]
        wur = [arC[:, 10240 + k * 2048: 10240 + (k + 1) * 2048] for k in range(3)]
        wdr = qhb.rearrange("p (k j m) -> p k j m", k=4, m=P)

        xn_free = None
        gb_free = [None, None]
        wg_free = [None, None, None]
        wu_free = [None, None, None]
        wd_free = [None, None, None, None]
        xf3_free = [None, None, None, None]
        s_free = [None, None, None, None]
        t2_free = [None, None]
        ob_free = [None, None]
        psGU_free = [None, None, None, None]
        psD_free = [None, None]
        xft = xf.rearrange("p a b d -> p (a b d)")
        xqb = [xft[:, i * 512:(i + 1) * 512] for i in range(4)]

        for seg in range(NSEG_Q):
            txn = sc.do("sync",
                        lambda e, seg=seg:
                        e.dma_start(out=xnseg[:, :, :],
                                    in_=xnqscr[:, :, seg * 512:(seg + 1) * 512].rearrange("c p t -> p c t")),
                        [xn_free] + (BAR2B if seg == 0 else []), "dS")
            fuse_toks = []
            for j in range(NJ):
                wsl = j % 3
                twg = sc.do("sync",
                            lambda e, j=j, wsl=wsl: e.dma_start(out=wgr[wsl][:, :], in_=wgbf[j]),
                            [wg_free[wsl]] + (BAR2B if seg == 0 and j < 3 else []), "dS")
                twu = sc.do("sync",
                            lambda e, j=j, wsl=wsl: e.dma_start(out=wur[wsl][:, :], in_=wubf[j]),
                            [wu_free[wsl]], "dS")
                bg = j % 2
                bankG = ps[bg]
                bankU = ps[2 + bg]

                def gmm(e, wsl=wsl, bankG=bankG):
                    w = wgr[wsl].rearrange("p (c m) -> p c m", m=P)
                    for c in range(NCD):
                        i_ = e.matmul(bankG, w[:, c, :], xnseg[:, c, :],
                                      start=(c == 0), stop=(c == NCD - 1))
                    return i_
                tgm = sc.do("tensor", gmm, (twg, txn, psGU_free[bg]), "sT")
                tga = sc.do("scalar",
                            lambda e, j=j, bg=bg, bankG=bankG:
                            e.activation(out=gbuf[:, bg, :], in_=bankG, func=AF.Silu,
                                         scale=sgc[:, j:j + 1]),
                            (tgm, gb_free[bg]), "sA")
                psGU_free[bg] = tga
                wg_free[wsl] = tgm

                def umm(e, wsl=wsl, bankU=bankU):
                    w = wur[wsl].rearrange("p (c m) -> p c m", m=P)
                    for c in range(NCD):
                        i_ = e.matmul(bankU, w[:, c, :], xnseg[:, c, :],
                                      start=(c == 0), stop=(c == NCD - 1))
                    return i_
                tum = sc.do("tensor", umm, (twu, txn, psGU_free[2 + bg]), "sT")
                wu_free[wsl] = tum
                tfu = sc.do("vector",
                            lambda e, j=j, bg=bg, bankU=bankU:
                            e.scalar_tensor_tensor(out=fuse[:, j, :], in0=bankU,
                                                   scalar=suc[:, j:j + 1], in1=gbuf[:, bg, :],
                                                   op0=ALU.mult, op1=ALU.mult),
                            (tum, tga, BAR2B if seg == 0 and j == 0 else None), "sV")
                psGU_free[2 + bg] = tfu
                gb_free[bg] = tfu
                fuse_toks.append(tfu)
            xn_free = fuse_toks[-1]

            for ob in range(NCD):
                dsl = ob % 2
                bankD = ps[4 + dsl]
                jg_mms = None
                for jg in range(8):
                    wdsl = jg % 4
                    twd = sc.do("sync",
                                lambda e, ob=ob, jg=jg, wdsl=wdsl:
                                e.dma_start(out=wdr[:, wdsl, :, :], in_=wdbf[ob, jg]),
                                [wd_free[wdsl]], "dS")

                    def dmm(e, jg=jg, wdsl=wdsl, bankD=bankD):
                        for jj in range(8):
                            i_ = e.matmul(bankD,
                                          wdr[:, wdsl, jj, :],
                                          fuse[:, jg * 8 + jj, :],
                                          start=(jg == 0 and jj == 0),
                                          stop=(jg == 7 and jj == 7))
                        return i_
                    deps = [twd] + (fuse_toks + [psD_free[dsl]] if jg == 0 else [])
                    tdm = sc.do("tensor", dmm, deps, "sT")
                    wd_free[wdsl] = tdm
                    jg_mms = tdm
                ksl = ob % 2
                tg_ = sc.do("scalar",
                            lambda e, ob=ob, ksl=ksl, bankD=bankD:
                            e.activation(out=gbuf[:, ksl, :], in_=bankD, func=AF.Silu,
                                         scale=sdc[:, ob:ob + 1]),
                            (jg_mms, gb_free[ksl]), "sA")
                psD_free[dsl] = tg_
                ssl4 = ob % 4
                tsl = sc.do("sync",
                            lambda e, seg=seg, ob=ob, ssl4=ssl4:
                            e.dma_start(out=sbufs[ssl4][:, :],
                                        in_=sscr[ob, :, seg * 512:(seg + 1) * 512]),
                            [s_free[ssl4], s_toks.get((ob, seg))], "dS")
                txq = sc.do("sync",
                            lambda e, seg=seg, ob=ob, ssl4=ssl4:
                            e.dma_start(out=xqb[ssl4][:, :],
                                        in_=dxqt[ob, :, seg * 512:(seg + 1) * 512]),
                            [xf3_free[ssl4]] + (BAR2B if seg == 0 and ob < 4 else []), "dS")
                t2 = sc.hz("vector",
                           lambda e, ksl=ksl, ssl4=ssl4:
                           e.tensor_mul(tmpg[:, ksl, :], gbuf[:, ksl, :], sbufs[ssl4][:, :]),
                           (tg_, t2_free[ksl], tsl))
                gb_free[ksl] = t2
                s_free[ssl4] = t2
                tout = sc.hz("vector",
                             lambda e, ksl=ksl, ssl4=ssl4:
                             e.tensor_add(outb[:, ksl, :], tmpg[:, ksl, :], xqb[ssl4][:, :]),
                             (t2, txq, ob_free[ksl]))
                t2_free[ksl] = tout
                xf3_free[ssl4] = tout
                tst = sc.do("scalar",
                            lambda e, seg=seg, ob=ob, ksl=ksl:
                            e.dma_start(out=dout[ob, :, seg * 512:(seg + 1) * 512],
                                        in_=outb[:, ksl, :]),
                            (tout,), "dA")
                ob_free[ksl] = tst

                FIN = dict(sc.bar())
        sc.ops["sync"].append(([(s, FIN[s]) for s in DMA_SEM_NAMES if FIN[s] > 0],
                               None, None, 0))
        sc.check()

        def replay(name, e):
            for waits, fn, sig, inc in sc.ops[name]:
                for s, v in waits:
                    e.wait_ge(SEM[s], v)
                if fn is None:
                    continue
                inst = fn(e)
                if sig is not None:
                    inst.then_inc(SEM[sig], inc)

        with nc.Block() as block:
            @block.sync
            def _(e):
                replay("sync", e)

            @block.gpsimd
            def _(e):
                replay("gpsimd", e)

            @block.scalar
            def _(e):
                replay("scalar", e)

            @block.vector
            def _(e):
                replay("vector", e)

            @block.tensor
            def _(e):
                replay("tensor", e)

    return nc


_NC_CACHE = None


def _get_nc():
    global _NC_CACHE
    if _NC_CACHE is None:
        _NC_CACHE = build_program()
    return _NC_CACHE


def kernel(L0_attn_x, L0_attn_Wq, L0_attn_bq, L0_attn_Wk, L0_attn_bk,
           L0_attn_Wv, L0_attn_bv, L0_attn_Wo, L0_attn_bo,
           L0_ffn_w_gate, L0_ffn_w_up, L0_ffn_w_down,
           L0_ffn_s_gate, L0_ffn_s_up, L0_ffn_s_down):
    x = np.ascontiguousarray(np.asarray(L0_attn_x, dtype=np.float32))
    B = x.shape[0]
    asf = lambda a: np.ascontiguousarray(np.asarray(a, dtype=np.float32))
    wq = asf(L0_attn_Wq).reshape(NCD, P, D)
    wk = asf(L0_attn_Wk).reshape(NCD, P, D)
    wv = asf(L0_attn_Wv).reshape(NCD, P, D)
    wo = asf(L0_attn_Wo).reshape(NCD, P, D)
    bq = asf(L0_attn_bq).reshape(NCD, P)
    bk = asf(L0_attn_bk).reshape(NCD, P)
    bv = asf(L0_attn_bv).reshape(NCD, P)
    bo = asf(L0_attn_bo).reshape(NCD, P)
    wg8 = np.ascontiguousarray(
        np.asarray(L0_ffn_w_gate, dtype=np.int8).reshape(NCD, P, NJ, P).transpose(2, 1, 0, 3))
    wu8 = np.ascontiguousarray(
        np.asarray(L0_ffn_w_up, dtype=np.int8).reshape(NCD, P, NJ, P).transpose(2, 1, 0, 3))
    wd8 = np.ascontiguousarray(
        np.asarray(L0_ffn_w_down, dtype=np.int8)
        .reshape(8, 8, P, NCD, P).transpose(3, 0, 2, 1, 4).reshape(NCD, 8, P, 1024))
    sg = asf(L0_ffn_s_gate).reshape(NJ, P)
    su = asf(L0_ffn_s_up).reshape(NJ, P)
    sd = asf(L0_ffn_s_down).reshape(NCD, P)

    jj = np.arange(512)[None, :]
    pp = np.arange(P)[:, None]
    qmasks = []
    qmts = []
    for h in range(2):
        m = np.zeros((P, NQI * 512), dtype=np.float32)
        for qi in range(NQI):
            r = (2 * qi + h) % 4
            m[:, qi * 512:(qi + 1) * 512] = np.where(jj <= (pp + 128 * r), 0.0, -30000.0)
        qmasks.append(m.astype(ml_dtypes.bfloat16))
        kk = np.arange(P)[:, None]
        qq = np.arange(P)[None, :]
        mt = np.zeros((P, 256), dtype=np.float32)
        for s_off in range(2):
            vis = kk <= ((h - s_off) * 128 + qq)
            mt[:, s_off * 128:(s_off + 1) * 128] = np.where(vis, 0.0, -30000.0)
        qmts.append(mt.astype(ml_dtypes.bfloat16))

    nc = _get_nc()
    in_maps = []
    for core in range(8):
        b, h = divmod(core, 2)
        xb = np.ascontiguousarray(x[b])
        xqb = np.ascontiguousarray(x[b].reshape(32, P, D)[h::2].reshape(NQ, D))
        xqtb = np.ascontiguousarray(xqb.T.reshape(NCD, P, NQ))
        in_maps.append({
            "x": xb, "xq": xqb, "xqt": xqtb, "qmask": qmasks[h], "qmt": qmts[h],
            "wq": wq, "wk": wk, "wv": wv, "wo": wo,
            "bq": bq, "bk": bk, "bv": bv, "bo": bo,
            "wg8": wg8, "wu8": wu8, "wd8": wd8,
            "sg": sg, "su": su, "sd": sd,
        })
    res = run_bass_kernel_spmd(nc, in_maps, core_ids=list(range(8)))
    y = np.empty((B, S, D), dtype=np.float32)
    for core in range(8):
        b, h = divmod(core, 2)
        o = np.asarray(res.results[core]["out"], dtype=np.float32)
        o = o.reshape(D, NQ).T.reshape(NQI, P, D)
        y[b].reshape(32, P, D)[h::2] = o
    return y


if __name__ == "__main__":
    nc = build_program()
    n = sum(1 for _ in nc.all_instructions())
    print("instructions:", n)


# version 12
# speedup vs baseline: 1.0122x; 1.0122x over previous
import numpy as np
import ml_dtypes
from contextlib import ExitStack

import concourse.bass as bass
import concourse.mybir as mybir
from concourse.bass_utils import run_bass_kernel_spmd

F32 = mybir.dt.float32
BF16 = mybir.dt.bfloat16
I8 = mybir.dt.int8
AF = mybir.ActivationFunctionType
ALU = mybir.AluOpType
AX = mybir.AxisListType

P = 128
D = 2048
NCD = 16
NH = 16
S = 4096
NSEG_K = 8
NQ = 2048
NSEG_Q = 4
NQI = 16
NKC = [qi // 2 + 1 for qi in range(NQI)]
DI = 8192
NJ = 64
ISQ = 0.08838834764831845
EPS = 1e-5

ENGINES = ("sync", "gpsimd", "scalar", "vector", "tensor")
DMAQ = {"dS": ("sync", 6), "dA": ("scalar", 4), "dG": ("gpsimd", 3),
        "dW": ("gpsimd", 2)}
DMA_SEM_NAMES = [f"{q}{i}" for q, (_, n) in DMAQ.items() for i in range(n)]
ALL_SEMS = tuple(DMA_SEM_NAMES) + ("sT", "sV", "sA", "hV")
OWN = {"sync": {f"dS{i}" for i in range(6)},
       "gpsimd": {f"dG{i}" for i in range(3)} | {f"dW{i}" for i in range(2)},
       "scalar": {"sA"} | {f"dA{i}" for i in range(4)},
       "vector": {"sV"}, "tensor": {"sT"}}


def _flat(deps, out=None):
    if out is None:
        out = []
    for d in deps:
        if d is None:
            continue
        if isinstance(d, tuple) and len(d) == 2 and isinstance(d[0], str):
            out.append(d)
        else:
            _flat(d, out)
    return out


class Sched:
    def __init__(self):
        self.ops = {e: [] for e in ENGINES}
        self.cnt = {s: 0 for s in ALL_SEMS}
        self.maxwait = {s: 0 for s in ALL_SEMS}
        self.rot = {q: 0 for q in DMAQ}

    def do(self, eng, fn, deps=(), sig=None, hard=()):
        waits = {}
        for s, v in _flat(deps):
            if s in OWN[eng]:
                continue
            if waits.get(s, -1) < v:
                waits[s] = v
        for s, v in _flat(hard):
            if waits.get(s, -1) < v:
                waits[s] = v
        if sig in DMAQ:
            q = sig
            sig = f"{q}{self.rot[q] % DMAQ[q][1]}"
            self.rot[q] += 1
            prev = self.cnt[sig]
            if prev > 0 and waits.get(sig, -1) < prev:
                waits[sig] = prev
        for s, v in waits.items():
            if self.maxwait[s] < v:
                self.maxwait[s] = v
        tok = None
        inc = 0
        if sig is not None:
            inc = 16 if sig in DMA_SEM_NAMES else 1
            self.cnt[sig] += inc
            tok = (sig, self.cnt[sig])
        self.ops[eng].append((sorted(waits.items()), fn, sig, inc))
        return tok

    def hz(self, eng, fn, deps=()):
        assert eng == "vector"
        tok = self.do(eng, fn, deps, sig="hV")
        self.ops[eng].append(([("hV", tok[1])], None, None, 0))
        self.maxwait["hV"] = max(self.maxwait["hV"], tok[1])
        return tok

    def bar(self):
        return [(s, self.cnt[s]) for s in ALL_SEMS if not s.startswith("dW")]

    def check(self):
        for s in ALL_SEMS:
            assert self.maxwait[s] <= self.cnt[s], (s, self.maxwait[s], self.cnt[s])


def build_program(dbg=False):
    nc = bass.Bass("TRN2", target_bir_lowering=False)
    SCR = "ExternalOutput" if dbg else "Internal"

    dx = nc.dram_tensor("x", [S, D], F32, kind="ExternalInput").ap()
    dxq = nc.dram_tensor("xq", [NQ, D], F32, kind="ExternalInput").ap()
    dxqt = nc.dram_tensor("xqt", [NCD, P, NQ], F32, kind="ExternalInput").ap()
    dqm = nc.dram_tensor("qmask", [P, NQI * 512], BF16, kind="ExternalInput").ap()
    dqmt = nc.dram_tensor("qmt", [P, 256], BF16, kind="ExternalInput").ap()
    dwq = nc.dram_tensor("wq", [NCD, P, D], F32, kind="ExternalInput").ap()
    dwk = nc.dram_tensor("wk", [NCD, P, D], F32, kind="ExternalInput").ap()
    dwv = nc.dram_tensor("wv", [NCD, P, D], F32, kind="ExternalInput").ap()
    dwo = nc.dram_tensor("wo", [NCD, P, D], F32, kind="ExternalInput").ap()
    dbq = nc.dram_tensor("bq", [NCD, P], F32, kind="ExternalInput").ap()
    dbk = nc.dram_tensor("bk", [NCD, P], F32, kind="ExternalInput").ap()
    dbv = nc.dram_tensor("bv", [NCD, P], F32, kind="ExternalInput").ap()
    dbo = nc.dram_tensor("bo", [NCD, P], F32, kind="ExternalInput").ap()
    dwg = nc.dram_tensor("wg8", [NJ, P, NCD, P], I8, kind="ExternalInput").ap()
    dwu = nc.dram_tensor("wu8", [NJ, P, NCD, P], I8, kind="ExternalInput").ap()
    dwd = nc.dram_tensor("wd8", [NCD, 8, P, 1024], I8, kind="ExternalInput").ap()
    dsg = nc.dram_tensor("sg", [NJ, P], F32, kind="ExternalInput").ap()
    dsu = nc.dram_tensor("su", [NJ, P], F32, kind="ExternalInput").ap()
    dsd = nc.dram_tensor("sd", [NCD, P], F32, kind="ExternalInput").ap()
    dout = nc.dram_tensor("out", [NCD, P, NQ], F32, kind="ExternalOutput").ap()

    xnscr = nc.dram_tensor("xnscr", [NCD, P, S], BF16, kind=SCR).ap()
    xnqscr = nc.dram_tensor("xnqscr", [NCD, P, NQ], BF16, kind=SCR).ap()
    kscr = nc.dram_tensor("kscr", [NH, P, S], BF16, kind=SCR).ap()
    vscr = nc.dram_tensor("vscr", [S, D], BF16, kind=SCR).ap()
    qscr = nc.dram_tensor("qscr", [NH, P, NQ], BF16, kind=SCR).ap()
    sscr = nc.dram_tensor("sscr", [NQI, P, D], BF16, kind=SCR).ap()
    fusescr = nc.dram_tensor("fusescr", [NSEG_Q, P, NJ * 512], BF16, kind=SCR).ap()
    dwnscr = nc.dram_tensor("dwnscr", [NSEG_Q, 2, 2, 4, P, 512], F32, kind=SCR).ap()
    wgbf = nc.dram_tensor("wgbf", [NJ, P, D], BF16, kind="Internal").ap()
    wubf = nc.dram_tensor("wubf", [NJ, P, D], BF16, kind="Internal").ap()
    wdbf = nc.dram_tensor("wdbf", [NCD, 8, P, 1024], BF16, kind="Internal").ap()

    with ExitStack() as st:
        en = st.enter_context
        idb = en(nc.sbuf_tensor("idb", [P, P], BF16))
        id32 = en(nc.sbuf_tensor("id32", [P, P], F32))
        qmb = en(nc.sbuf_tensor("qmb", [P, NQI * 512], BF16))
        konst = en(nc.sbuf_tensor("konst", [P, 4], F32))
        bqs = en(nc.sbuf_tensor("bqs", [P, NCD], F32))
        bks = en(nc.sbuf_tensor("bks", [P, NCD], F32))
        bvs = en(nc.sbuf_tensor("bvs", [P, NCD], F32))
        sgc = en(nc.sbuf_tensor("sgc", [P, NJ], F32))
        suc = en(nc.sbuf_tensor("suc", [P, NJ], F32))
        boc = en(nc.sbuf_tensor("boc", [P, NCD], F32))
        sdc = en(nc.sbuf_tensor("sdc", [P, NCD], F32))
        xf = en(nc.sbuf_tensor("xf", [P, 2, 2, D], F32))
        outb = en(nc.sbuf_tensor("outb", [P, 2, 512], F32))
        tmpg = en(nc.sbuf_tensor("tmpg", [P, 2, 512], F32))
        stats = en(nc.sbuf_tensor("stats", [P, 64], F32))
        lnbuf = en(nc.sbuf_tensor("lnbuf", [P, 8192], BF16))
        arA = en(nc.sbuf_tensor("arA", [P, 32768], BF16))
        arC = en(nc.sbuf_tensor("arC", [P, 16384], BF16))
        qhb = en(nc.sbuf_tensor("qhb", [P, 4096], BF16))
        pstg = en(nc.sbuf_tensor("pstg", [P, 2, 1024], BF16))
        qmtb = en(nc.sbuf_tensor("qmtb", [P, 256], BF16))
        onesb = en(nc.sbuf_tensor("onesb", [1, P], BF16))
        onek = en(nc.sbuf_tensor("onek", [P, 1], BF16))
        nmrow = en(nc.sbuf_tensor("nmrow", [1, 2, P], BF16))
        pon = en(nc.sbuf_tensor("pon", [P, 2, P], BF16))
        gbuf = en(nc.sbuf_tensor("gbuf", [P, 2, 512], BF16))

        psA = en(nc.psum_tensor("psA", [P, 1024], F32))
        psE = en(nc.psum_tensor("psE", [P, 1024], F32))
        psO0 = en(nc.psum_tensor("psO0", [P, 512], F32))
        psO1 = en(nc.psum_tensor("psO1", [P, 512], F32))
        psB = en(nc.psum_tensor("psB", [P, 1024], BF16))
        psB2 = en(nc.psum_tensor("psB2", [P, 1024], BF16))
        ps = [psA[:, 0:512], psA[:, 512:1024], psE[:, 0:512], psE[:, 512:1024],
              psO0[:, 0:512], psO1[:, 0:512]]
        psOP = [psO0, psO1]

        vr_bo = xf[0:16, 1, 0, 0:128]
        vr_sd = xf[0:16, 1, 0, 128:256]
        vr_bq = xf[0:16, 1, 1, 0:128]
        vr_bk = xf[0:16, 1, 1, 128:256]
        vr_bv = xf[0:16, 1, 1, 256:384]
        vr_sg = xf[0:64, 0, 0, 0:128]
        vr_su = xf[0:64, 0, 0, 128:256]
        sqscr = qhb[:, 0:2048]

        SEM = {s: en(nc.semaphore(s)) for s in ALL_SEMS}
        sc = Sched()

        TRB = [(psB, psB[:, 0:512]), (psB2, psB2[:, 0:512])]

        sc.do("sync", lambda e: e.dma_start(out=qmb[:], in_=dqm[:, :]), (), "dS")
        sc.do("sync", lambda e: e.dma_start(out=qmtb[:], in_=dqmt[:, :]), (), "dS")
        sc.do("sync", lambda e: e.dma_start(out=vr_bq, in_=dbq[:, :]), (), "dS")
        sc.do("sync", lambda e: e.dma_start(out=vr_bk, in_=dbk[:, :]), (), "dS")
        sc.do("sync", lambda e: e.dma_start(out=vr_bv, in_=dbv[:, :]), (), "dS")
        sc.do("sync", lambda e: e.dma_start(out=vr_sg, in_=dsg[:, :]), (), "dS")
        sc.do("sync", lambda e: e.dma_start(out=vr_su, in_=dsu[:, :]), (), "dS")
        sc.do("sync", lambda e: e.dma_start(out=vr_bo, in_=dbo[:, :]), (), "dS")
        t_ld0 = sc.do("sync", lambda e: e.dma_start(out=vr_sd, in_=dsd[:, :]), (), "dS")

        def _setup_g(e):
            e.memset(idb[:], 1.0)
            e.affine_select(idb[:], idb[:], [[-1, P]], ALU.is_equal, 0.0,
                            base=0, channel_multiplier=1)
            e.memset(id32[:], 1.0)
            e.affine_select(id32[:], id32[:], [[-1, P]], ALU.is_equal, 0.0,
                            base=0, channel_multiplier=1)
            e.memset(onesb[:], 1.0)
            e.memset(onek[:], 1.0)
            return e.memset(konst[:, 0:1], EPS)
        t_g0 = sc.do("gpsimd", _setup_g, (), "sV")

        conv_tok = {}
        for nm, (src, dst, fl) in (("g", (dwg, wgbf, True)), ("u", (dwu, wubf, True)),
                                   ("d", (dwd, wdbf, False))):
            for q4 in range(4):
                a, b_ = q4 * 4, (q4 + 1) * 4
                if fl:
                    a, b_ = q4 * (NJ // 4), (q4 + 1) * (NJ // 4)
                    fn = (lambda e, s_=src, d_=dst, a=a, b=b_:
                          e.dma_start(out=d_[a:b], in_=s_[a:b].rearrange("a p c m -> a p (c m)")))
                else:
                    fn = (lambda e, s_=src, d_=dst, a=a, b=b_:
                          e.dma_start(out=d_[a:b], in_=s_[a:b].rearrange("a g p m -> a g p m")))
                conv_tok[(nm, q4)] = sc.do("gpsimd", fn, (), "dW")

        t_tr = sc.do("tensor", lambda e: e.transpose(ps[0][:, 0:16], vr_bq, id32[0:16, 0:16]), (t_ld0, t_g0), "sT")
        t_tr = sc.do("tensor", lambda e: e.transpose(ps[0][:, 16:32], vr_bk, id32[0:16, 0:16]), (), "sT")
        t_tr = sc.do("tensor", lambda e: e.transpose(ps[0][:, 32:48], vr_bv, id32[0:16, 0:16]), (), "sT")
        t_tr = sc.do("tensor", lambda e: e.transpose(ps[0][:, 64:128], vr_sg, id32[0:64, 0:64]), (), "sT")
        t_tr = sc.do("tensor", lambda e: e.transpose(ps[0][:, 128:192], vr_su, id32[0:64, 0:64]), (), "sT")
        t_tr = sc.do("tensor", lambda e: e.transpose(ps[0][:, 192:208], vr_bo, id32[0:16, 0:16]), (), "sT")
        t_tr = sc.do("tensor", lambda e: e.transpose(ps[0][:, 208:224], vr_sd, id32[0:16, 0:16]), (), "sT")
        sc.do("scalar", lambda e: e.activation(out=bqs[:, :], in_=ps[0][:, 0:16], func=AF.Copy, scale=ISQ), (t_tr,), "sA")
        sc.do("scalar", lambda e: e.copy(bks[:, :], ps[0][:, 16:32]), (), "sA")
        sc.do("scalar", lambda e: e.copy(bvs[:, :], ps[0][:, 32:48]), (), "sA")
        sc.do("scalar", lambda e: e.copy(sgc[:, :], ps[0][:, 64:128]), (), "sA")
        sc.do("scalar", lambda e: e.copy(suc[:, :], ps[0][:, 128:192]), (), "sA")
        sc.do("scalar", lambda e: e.copy(boc[:, :], ps[0][:, 192:208]), (), "sA")
        sc.do("scalar", lambda e: e.copy(sdc[:, :], ps[0][:, 208:224]), (), "sA")

        BAR0 = sc.bar()

        xnstage = [arA[:, 0:8192].rearrange("p (c t) -> p c t", t=512),
                   arA[:, 8192:16384].rearrange("p (c t) -> p c t", t=512)]
        xntm = [lnbuf[:, 0:2048], lnbuf[:, 2048:4096]]
        grp_i = [0]

        def emit_ln(src_ap, dst_scr, ntok, bar):
            nhs = ntok // 256
            xf_free = [None, None]
            stage_free = [None, None]
            trb_free = [None, None]
            hist = {}
            store_toks = []
            for hsg in range(nhs):
                sl = hsg % 2
                seg = hsg // 2
                ssl = seg % 2
                tld = sc.do("sync",
                            lambda e, sl=sl, o=hsg * 256, src_ap=src_ap:
                            e.dma_start(out=xf[:, sl, :, :],
                                        in_=src_ap[o:o + 256, :].rearrange("(n p) d -> p n d", p=P)),
                            [xf_free[sl]] + (bar if hsg < 2 else []), "dS")
                xf_readers = []
                for i in range(2):
                    tt = hsg * 2 + i
                    c0 = (tt % 4) * 8
                    xin = xf[:, sl, i, :]
                    old = hist.get(tt - 4, {})
                    tsq = sc.do("scalar",
                                lambda e, xin=xin, c0=c0:
                                e.activation(out=sqscr[:, :], in_=xin, func=AF.Square,
                                             accum_out=stats[:, c0:c0 + 1]),
                                (tld, old.get("var")) + (tuple(bar) if tt < 2 else ()), "sA")
                    tns = sc.do("vector",
                                lambda e, xin=xin, c0=c0:
                                e.reduce_sum(stats[:, c0 + 1:c0 + 2], xin, axis=AX.X, negate=True),
                                (tld, old.get("nm"), old.get("ms")), "sV")
                    tnm = sc.do("scalar",
                                lambda e, c0=c0:
                                e.activation(out=stats[:, c0 + 2:c0 + 3], in_=stats[:, c0 + 1:c0 + 2],
                                             func=AF.Copy, scale=1.0 / D),
                                (tns, old.get("xn")), "sA")
                    tms = sc.do("scalar",
                                lambda e, c0=c0:
                                e.activation(out=stats[:, c0 + 3:c0 + 4], in_=stats[:, c0 + 1:c0 + 2],
                                             func=AF.Square, scale=1.0 / D),
                                (tns, old.get("var")), "sA")
                    tvar = sc.do("vector",
                                 lambda e, c0=c0:
                                 e.scalar_tensor_tensor(out=stats[:, c0 + 4:c0 + 5],
                                                        in0=stats[:, c0:c0 + 1], scalar=1.0 / D,
                                                        in1=stats[:, c0 + 3:c0 + 4],
                                                        op0=ALU.mult, op1=ALU.subtract),
                                 (tsq, tms, old.get("std")), "sV")
                    tstd = sc.do("scalar",
                                 lambda e, c0=c0:
                                 e.activation(out=stats[:, c0 + 5:c0 + 6], in_=stats[:, c0 + 4:c0 + 5],
                                              func=AF.Sqrt, bias=konst[:, 0:1]),
                                 (tvar, old.get("ri")), "sA")
                    tri = sc.hz("vector",
                                lambda e, c0=c0:
                                e.reciprocal(stats[:, c0 + 6:c0 + 7], stats[:, c0 + 5:c0 + 6]),
                                (tstd,))
                    xsl = tt % 2
                    txn = sc.do("vector",
                                lambda e, xin=xin, c0=c0, xsl=xsl:
                                e.tensor_scalar(out=xntm[xsl][:, :], in0=xin,
                                                scalar1=stats[:, c0 + 2:c0 + 3],
                                                scalar2=stats[:, c0 + 6:c0 + 7],
                                                op0=ALU.add, op1=ALU.mult),
                                (tri, tnm, hist.get(tt - 2, {}).get("tr_last")), "sV")
                    xf_readers += [tsq, tns, txn]
                    evs = []
                    trl = None
                    for g in range(4):
                        gsl = grp_i[0] % 2
                        grp_i[0] += 1
                        bank, bslice = TRB[gsl]

                        def trfn(e, xsl=xsl, g=g, bslice=bslice):
                            for cc in range(4):
                                c = g * 4 + cc
                                i_ = e.transpose(bslice[:, cc * 128:(cc + 1) * 128],
                                                 xntm[xsl][:, c * 128:(c + 1) * 128], idb[:])
                            return i_
                        ttr = sc.do("tensor", trfn, (txn, trb_free[gsl], t_g0), "sT")
                        trl = ttr
                        tev = sc.do("vector",
                                    lambda e, ssl=ssl, tt=tt, g=g, bslice=bslice:
                                    e.tensor_copy(xnstage[ssl][:, g * 4:(g + 1) * 4, (tt % 4) * 128:(tt % 4 + 1) * 128],
                                                  bslice.rearrange("p (c t) -> p c t", t=P)),
                                    (ttr, stage_free[ssl]), "sV")
                        trb_free[gsl] = tev
                        evs.append(tev)
                    hist[tt] = {"ns": tns, "nm": tnm, "ms": tms, "var": tvar,
                                "std": tstd, "ri": tri, "xn": txn,
                                "tr_last": trl, "evs": evs}
                xf_free[sl] = xf_readers
                if hsg % 2 == 1:
                    all_evs = (hist[hsg * 2 - 2]["evs"] + hist[hsg * 2 - 1]["evs"] +
                               hist[hsg * 2]["evs"] + hist[hsg * 2 + 1]["evs"])
                    tst = sc.do("scalar",
                                lambda e, seg=seg, ssl=ssl, dst_scr=dst_scr:
                                e.dma_start(out=dst_scr[:, :, seg * 512:(seg + 1) * 512].rearrange("c p t -> p c t"),
                                            in_=xnstage[ssl][:, :, :]),
                                all_evs, "dA")
                    stage_free[ssl] = tst
                    store_toks.append(tst)
            return store_toks

        emit_ln(dx, xnscr, S, BAR0)
        emit_ln(dxq, xnqscr, NQ, sc.bar())
        BAR1a = sc.bar()

        wres = arC[:, 0:16384].rearrange("p (c m) -> p c m", m=1024)
        xnb = [arA[:, 16384:24576].rearrange("p (c t) -> p c t", t=512),
               arA[:, 24576:32768].rearrange("p (c t) -> p c t", t=512)]
        kstage = [arA[:, 0:4096].rearrange("p (h t) -> p h t", t=512),
                  arA[:, 4096:8192].rearrange("p (h t) -> p h t", t=512)]
        vstage = [arA[:, 0:1024], arA[:, 1024:2048]]

        def emit_kq(wsrc, biascol, scale, src_scr, ntokens, dst_scr, bar):
            nseg = ntokens // 512
            wres_free = bar
            xnb_free = [None, None]
            bank_free = [None, None]
            kst_free = [None, None]
            last_toks = []
            for p_ in range(2):
                tw = sc.do("gpsimd",
                           lambda e, p_=p_, wsrc=wsrc:
                           e.dma_start(out=wres[:, :, :],
                                       in_=wsrc[:, :, p_ * 1024:(p_ + 1) * 1024].rearrange("c p m -> p c m")),
                           [wres_free] + (bar if p_ == 0 else []), "dG")
                for seg in range(nseg):
                    sl = seg % 2
                    tx = sc.do("sync",
                               lambda e, seg=seg, sl=sl, src_scr=src_scr:
                               e.dma_start(out=xnb[sl][:, :, :],
                                           in_=src_scr[:, :, seg * 512:(seg + 1) * 512].rearrange("c p t -> p c t")),
                               [xnb_free[sl]] + (bar if p_ == 0 and seg < 2 else []), "dS")
                    last_ev = None
                    for hh in range(8):
                        bsl = hh % 2
                        bank = ps[bsl]

                        def mmfn(e, sl=sl, hh=hh, bank=bank):
                            for c in range(NCD):
                                i_ = e.matmul(bank[:, 0:512],
                                              wres[:, c, hh * 128:(hh + 1) * 128],
                                              xnb[sl][:, c, :],
                                              start=(c == 0), stop=(c == NCD - 1))
                            return i_
                        tmm = sc.do("tensor", mmfn, (tw, tx, bank_free[bsl]), "sT")
                        gh = p_ * 8 + hh
                        tev = sc.do("scalar",
                                    lambda e, sl=sl, hh=hh, bank=bank, gh=gh, biascol=biascol, scale=scale:
                                    e.activation(out=kstage[sl][:, hh, :], in_=bank[:, 0:512],
                                                 func=AF.Identity, bias=biascol[:, gh:gh + 1], scale=scale),
                                    (tmm, kst_free[sl]), "sA")
                        bank_free[bsl] = tev
                        last_ev = tev
                    tst = sc.do("scalar",
                                lambda e, p_=p_, seg=seg, sl=sl, dst_scr=dst_scr:
                                e.dma_start(out=dst_scr[p_ * 8:(p_ + 1) * 8, :, seg * 512:(seg + 1) * 512].rearrange("h p t -> p h t"),
                                            in_=kstage[sl][:, :, :]),
                                (), "dA", hard=(last_ev,))
                    kst_free[sl] = tst
                    xnb_free[sl] = tst
                    last_toks.append(tst)
                wres_free = last_toks[-1]
            return last_toks

        def emit_v(bar):
            wres_free = bar
            xnb_free = [None, None]
            bank_free = [None, None]
            vst_free = [None, None]
            last_toks = []
            for p_ in range(2):
                tw = sc.do("gpsimd",
                           lambda e, p_=p_:
                           e.dma_start(out=wres[:, :, :],
                                       in_=dwv[:, :, p_ * 1024:(p_ + 1) * 1024].rearrange("c p m -> p c m")),
                           [wres_free] + (bar if p_ == 0 else []), "dG")
                for seg in range(NSEG_K):
                    sl = seg % 2
                    tx = sc.do("sync",
                               lambda e, seg=seg, sl=sl:
                               e.dma_start(out=xnb[sl][:, :, :],
                                           in_=xnscr[:, :, seg * 512:(seg + 1) * 512].rearrange("c p t -> p c t")),
                               [xnb_free[sl]] + (bar if p_ == 0 and seg < 2 else []), "dS")
                    seg_toks = []
                    for sub in range(4):
                        vsl = sub % 2
                        evs = []
                        for dt in range(2):
                            bank = ps[2 + dt]

                            def mmfn(e, sl=sl, sub=sub, dt=dt, bank=bank):
                                for c in range(NCD):
                                    i_ = e.matmul(bank[:, 0:512],
                                                  xnb[sl][:, c, sub * 128:(sub + 1) * 128],
                                                  wres[:, c, dt * 512:(dt + 1) * 512],
                                                  start=(c == 0), stop=(c == NCD - 1))
                                return i_
                            tmm = sc.do("tensor", mmfn, (tw, tx, bank_free[dt]), "sT")
                            tev = sc.do("vector",
                                        lambda e, vsl=vsl, dt=dt, bank=bank:
                                        e.tensor_copy(vstage[vsl][:, dt * 512:(dt + 1) * 512], bank[:, 0:512]),
                                        (tmm, vst_free[vsl]), "sV")
                            bank_free[dt] = tev
                            evs.append(tev)
                        tstv = sc.do("scalar",
                                     lambda e, p_=p_, seg=seg, sub=sub, vsl=vsl:
                                     e.dma_start(out=vscr[seg * 512 + sub * 128: seg * 512 + (sub + 1) * 128,
                                                          p_ * 1024:(p_ + 1) * 1024],
                                                 in_=vstage[vsl][:, :]),
                                     evs, "dA")
                        vst_free[vsl] = tstv
                        seg_toks.append(tstv)
                    xnb_free[sl] = seg_toks
                    last_toks += seg_toks
                wres_free = last_toks[-1]
            return last_toks

        emit_kq(dwk, bks, 1.0, xnscr, S, kscr, BAR1a)
        emit_v(sc.bar())
        emit_kq(dwq, bqs, ISQ, xnqscr, NQ, qscr, sc.bar())
        BAR1 = sc.bar()

        kbuf = [arC[:, 0:4096], arC[:, 4096:8192]]
        vbuf = [arC[:, 8192:12288].rearrange("p (n d) -> p n d", d=P),
                arC[:, 12288:16384].rearrange("p (n d) -> p n d", d=P)]
        qbuf = [qhb[:, 0:2048], qhb[:, 2048:4096]]
        attn = arA
        NKS = [2 * qi + 2 for qi in range(NQI)]
        NHM = [(NKS[qi] + 3) // 4 for qi in range(NQI)]

        kq_free = [None, None]
        v_free = [None, None]
        psA_free = [None, None]
        psE_free = [None, None]
        pstf2 = pstg.rearrange("p a t -> p (a t)")
        pstg4 = [pstf2[:, i * 512:(i + 1) * 512] for i in range(4)]
        pst_free = [None, None, None, None]
        psO_free = [None, None]
        trb_free = [None, None]
        pon_free = [None, None]
        nmcol_free = [None, None]
        nmrow_free = [None, None]
        nmrow_tok = [None, None]
        gm = [0]
        pm = [0]

        def p2cols(par):
            b = 32 * par
            return dict(cm=b, nm=b + 16)

        for h in range(NH):
            hl = h % 2
            dep_ld = [kq_free[hl]] + (BAR1 if h < 2 else [])
            tk = sc.do("sync",
                       lambda e, h=h, hl=hl: e.dma_start(out=kbuf[hl][:, :], in_=kscr[h]),
                       dep_ld, "dS")
            tq = sc.do("sync",
                       lambda e, h=h, hl=hl: e.dma_start(out=qbuf[hl][:, :], in_=qscr[h]),
                       dep_ld, "dS")
            tvl = sc.do("sync",
                        lambda e, h=h, hl=hl:
                        e.dma_start(out=vbuf[hl][:, :, :],
                                    in_=vscr.rearrange("(n p) d -> p n d", p=P)[:, :, h * 128:(h + 1) * 128]),
                        [v_free[hl]] + (BAR1 if h < 2 else []), "dS")
            kq_readers = []
            v_readers = []
            st1 = {}
            pend = [None]

            def p1_chunk(qi, kc):
                par = qi % 2
                cols = p2cols(par)
                nkc = NKC[qi]
                bsl = kc % 2
                bank = psA[:, bsl * 512:(bsl + 1) * 512]
                last = (kc == nkc - 1)
                tmm = sc.do("tensor",
                            lambda e, qi=qi, kc=kc, hl=hl, bank=bank:
                            e.matmul(bank, qbuf[hl][:, qi * 128:(qi + 1) * 128],
                                     kbuf[hl][:, kc * 512:(kc + 1) * 512],
                                     start=True, stop=True),
                            (tk, tq, psA_free[bsl]), "sT")
                kq_readers.append(tmm)
                dep = tmm
                if last:
                    dep = sc.hz("vector",
                                lambda e, qi=qi, bank=bank:
                                e.tensor_add(bank, bank, qmb[:, qi * 512:(qi + 1) * 512]),
                                (tmm,))
                    tred = sc.hz("vector",
                                 lambda e, kc=kc, bank=bank, cc=cols["cm"]:
                                 e.reduce_max(stats[:, cc + kc:cc + kc + 1], bank, axis=AX.X),
                                 (dep,))
                else:
                    tred = sc.do("vector",
                                 lambda e, kc=kc, bank=bank, cc=cols["cm"]:
                                 e.reduce_max(stats[:, cc + kc:cc + kc + 1], bank, axis=AX.X),
                                 (dep,), "sV")
                psA_free[bsl] = tred
                st1.setdefault(qi, []).append(tred)
                return tred

            def p1_tail(qi):
                par = qi % 2
                cols = p2cols(par)
                tnm = sc.do("vector",
                            lambda e, nkc=NKC[qi], cc=cols["cm"], cn=cols["nm"]:
                            e.reduce_max(stats[:, cn:cn + 1], stats[:, cc:cc + nkc],
                                         axis=AX.X, negate=True),
                            (st1[qi][-1], nmcol_free[par]), "sV")
                ttr = sc.do("tensor",
                            lambda e, par=par, cn=cols["nm"]:
                            e.transpose(psOP[par][0:1, 256:384],
                                        stats[:, cn:cn + 1], id32[:]),
                            (tnm, psO_free[par]), "sT")
                nmcol_free[par] = ttr
                tev = sc.do("scalar",
                            lambda e, par=par:
                            e.copy(nmrow[0:1, par, :], psOP[par][0:1, 256:384]),
                            (ttr, nmrow_free[par]), "sA")
                nmrow_tok[par] = tev
                st1[qi] = tev

            def p2_tail(qj, pv_last):
                par = qj % 2
                cols = p2cols(par)
                tli = sc.hz("vector",
                            lambda e, par=par, cn=cols["nm"]:
                            e.reciprocal(stats[:, cn + 1:cn + 2], psOP[par][:, 128:129]),
                            (pv_last,))
                tno = sc.do("vector",
                            lambda e, par=par, cn=cols["nm"]:
                            e.tensor_scalar_mul(pon[:, par, :], psOP[par][:, 0:128],
                                                stats[:, cn + 1:cn + 2]),
                            (tli, pon_free[par]), "sV")
                psO_free[par] = tno
                gsl = gm[0] % 2
                gm[0] += 1
                bankT, bslice = TRB[gsl]
                tot = sc.do("tensor",
                            lambda e, par=par, bslice=bslice:
                            e.transpose(bslice[:, 0:128], pon[:, par, :], idb[:]),
                            (tno, trb_free[gsl]), "sT")
                tfin = sc.do("scalar",
                             lambda e, h=h, qj=qj, bslice=bslice:
                             e.activation(out=attn[:, h * 2048 + qj * 128: h * 2048 + (qj + 1) * 128],
                                          in_=bslice[:, 0:128],
                                          func=AF.Identity, bias=bvs[:, h:h + 1]),
                             (tot,), "sA")
                trb_free[gsl] = tfin
                pon_free[par] = tot

            def flush_pend():
                if pend[0] is None:
                    return
                qj, hm, nks, psl, texp = pend[0]
                pend[0] = None
                par = qj % 2
                last = (hm == NHM[qj] - 1)

                def pvfn(e, qj=qj, hm=hm, nks=nks, psl=psl, hl=hl, par=par, last=last):
                    for ks in range(nks):
                        e.matmul(psOP[par][:, 0:128],
                                 pstg4[psl][:, ks * 128:(ks + 1) * 128],
                                 vbuf[hl][:, hm * 4 + ks, :],
                                 start=(hm == 0 and ks == 0), stop=False)
                        i_ = e.matmul(psOP[par][:, 128:129],
                                      pstg4[psl][:, ks * 128:(ks + 1) * 128],
                                      onek[:, 0:1],
                                      start=False, stop=(last and ks == nks - 1))
                    return i_
                tpv = sc.do("tensor", pvfn,
                            (texp, tvl) + ((psO_free[par],) if hm == 0 else ()), "sT")
                pst_free[psl] = tpv
                v_readers.append(tpv)
                if last:
                    p2_tail(qj, tpv)

            def p2_half(qj, hm):
                par = qj % 2
                nks_tot = NKS[qj]
                nks = min(4, nks_tot - 4 * hm)
                last = (hm == NHM[qj] - 1)
                esl = hm % 2
                bank = psE[:, esl * 512:(esl + 1) * 512]

                def mmfn(e, qj=qj, hm=hm, nks=nks, hl=hl, par=par, bank=bank):
                    for ks in range(nks):
                        ksg = hm * 4 + ks
                        e.matmul(bank[:, ks * 128:(ks + 1) * 128],
                                 kbuf[hl][:, ksg * 128:(ksg + 1) * 128],
                                 qbuf[hl][:, qj * 128:(qj + 1) * 128],
                                 start=(ks == 0), stop=False)
                    for ks in range(nks):
                        i_ = e.matmul(bank[:, ks * 128:(ks + 1) * 128],
                                      onesb[0:1, 0:P],
                                      nmrow[0:1, par, :],
                                      start=False, stop=(ks == nks - 1))
                    return i_
                tmm = sc.do("tensor", mmfn, (psE_free[esl], nmrow_tok[par], tk, tq), "sT")
                kq_readers.append(tmm)
                dep = tmm
                if last:
                    off2 = nks - 2
                    dep = sc.do("vector",
                                lambda e, off2=off2, bank=bank:
                                e.tensor_add(bank[:, off2 * 128:(off2 + 2) * 128],
                                             bank[:, off2 * 128:(off2 + 2) * 128],
                                             qmtb[:, :]),
                                (tmm,), "sV")
                psl = pm[0] % 4
                pm[0] += 1
                texp = sc.do("scalar",
                             lambda e, nks=nks, psl=psl, bank=bank:
                             e.activation(out=pstg4[psl][:, 0:nks * 128],
                                          in_=bank[:, 0:nks * 128], func=AF.Exp),
                             (dep, pst_free[psl]), "sA")
                psE_free[esl] = texp
                flush_pend()
                pend[0] = (qj, hm, nks, psl, texp)
                nmrow_free[par] = tmm
            for qi in range(NQI + 1):
                nA = NKC[qi] if qi < NQI else 0
                nB = NHM[qi - 1] if qi >= 1 else 0
                for i in range(max(nA, nB)):
                    if i < nA:
                        p1_chunk(qi, i)
                    if i < nB:
                        p2_half(qi - 1, i)
                if qi < NQI:
                    p1_tail(qi)
            flush_pend()
            kq_free[hl] = kq_readers
            v_free[hl] = v_readers
        BAR2 = sc.bar()

        sstg2 = [pstg.rearrange("p a t -> p (a t)")[:, i * 512:(i + 1) * 512] for i in range(4)]
        wres_free = BAR2
        bank_free = [None, None]
        sst_free = [None, None, None, None]
        s_toks = {}
        for p_ in range(2):
            tw = sc.do("gpsimd",
                       lambda e, p_=p_:
                       e.dma_start(out=wres[:, :, :],
                                   in_=dwo[:, :, p_ * 1024:(p_ + 1) * 1024].rearrange("c p m -> p c m")),
                       [wres_free] + (BAR2 if p_ == 0 else []), "dG")
            for g_ in range(4):
                for obl in range(8):
                    obg = p_ * 8 + obl
                    bsl = obl % 2
                    bank = ps[bsl]
                    ssl = (g_ * 8 + obl) % 4

                    def mmfn(e, g_=g_, obl=obl, bank=bank):
                        for c in range(NCD):
                            i_ = e.matmul(bank,
                                          wres[:, c, obl * 128:(obl + 1) * 128],
                                          attn[:, c * 2048 + g_ * 512: c * 2048 + (g_ + 1) * 512],
                                          start=(c == 0), stop=(c == NCD - 1))
                        return i_
                    tmm = sc.do("tensor", mmfn,
                                [tw, bank_free[bsl]] + (BAR2 if p_ == 0 and g_ == 0 and obl == 0 else []), "sT")
                    tev = sc.do("scalar",
                                lambda e, obg=obg, ssl=ssl, bank=bank:
                                e.activation(out=sstg2[ssl][:, :], in_=bank,
                                             func=AF.Identity, bias=boc[:, obg:obg + 1]),
                                (tmm, sst_free[ssl]), "sA")
                    bank_free[bsl] = tev
                    tst = sc.do("scalar",
                                lambda e, obg=obg, g_=g_, ssl=ssl:
                                e.dma_start(out=sscr[obg, :, g_ * 512:(g_ + 1) * 512],
                                            in_=sstg2[ssl][:, :]),
                                (), "dA", hard=(tev,))
                    sst_free[ssl] = tst
                    s_toks[(obg, g_)] = tst
            wres_free = tst
        BAR2B = sc.bar()

        fuse = arA.rearrange("p (j t) -> p j t", t=512)
        xnseg = lnbuf.rearrange("p (c t) -> p c t", t=512)
        sbufs = [arC[:, 0:512], arC[:, 512:1024], arC[:, 1024:1536], arC[:, 1536:2048]]
        wgr = [arC[:, 4096 + k * 2048: 4096 + (k + 1) * 2048] for k in range(3)]
        wur = [arC[:, 10240 + k * 2048: 10240 + (k + 1) * 2048] for k in range(3)]
        wdr = qhb.rearrange("p (k j m) -> p k j m", k=4, m=P)

        xn_free = None
        gb_free = [None, None]
        wg_free = [None, None, None]
        wu_free = [None, None, None]
        wd_free = [None, None, None, None]
        xf3_free = [None, None, None, None]
        s_free = [None, None, None, None]
        t2_free = [None, None]
        ob_free = [None, None]
        psGU_free = [None, None, None, None]
        psD_free = [None, None]
        xft = xf.rearrange("p a b d -> p (a b d)")
        xqb = [xft[:, i * 512:(i + 1) * 512] for i in range(4)]

        for seg in range(NSEG_Q):
            txn = sc.do("sync",
                        lambda e, seg=seg:
                        e.dma_start(out=xnseg[:, :, :],
                                    in_=xnqscr[:, :, seg * 512:(seg + 1) * 512].rearrange("c p t -> p c t")),
                        [xn_free] + (BAR2B if seg == 0 else []), "dS")
            fuse_toks = []
            for j in range(NJ):
                wsl = j % 3
                twg = sc.do("sync",
                            lambda e, j=j, wsl=wsl: e.dma_start(out=wgr[wsl][:, :], in_=wgbf[j]),
                            [wg_free[wsl], conv_tok[("g", j // 16)]] +
                            (BAR2B if seg == 0 and j < 3 else []), "dS")
                twu = sc.do("sync",
                            lambda e, j=j, wsl=wsl: e.dma_start(out=wur[wsl][:, :], in_=wubf[j]),
                            [wu_free[wsl], conv_tok[("u", j // 16)]], "dS")
                bg = j % 2
                bankG = ps[bg]
                bankU = ps[2 + bg]

                def gmm(e, wsl=wsl, bankG=bankG):
                    w = wgr[wsl].rearrange("p (c m) -> p c m", m=P)
                    for c in range(NCD):
                        i_ = e.matmul(bankG, w[:, c, :], xnseg[:, c, :],
                                      start=(c == 0), stop=(c == NCD - 1))
                    return i_
                tgm = sc.do("tensor", gmm, (twg, txn, psGU_free[bg]), "sT")
                tga = sc.do("scalar",
                            lambda e, j=j, bg=bg, bankG=bankG:
                            e.activation(out=gbuf[:, bg, :], in_=bankG, func=AF.Silu,
                                         scale=sgc[:, j:j + 1]),
                            (tgm, gb_free[bg]), "sA")
                psGU_free[bg] = tga
                wg_free[wsl] = tgm

                def umm(e, wsl=wsl, bankU=bankU):
                    w = wur[wsl].rearrange("p (c m) -> p c m", m=P)
                    for c in range(NCD):
                        i_ = e.matmul(bankU, w[:, c, :], xnseg[:, c, :],
                                      start=(c == 0), stop=(c == NCD - 1))
                    return i_
                tum = sc.do("tensor", umm, (twu, txn, psGU_free[2 + bg]), "sT")
                wu_free[wsl] = tum
                tfu = sc.do("vector",
                            lambda e, j=j, bg=bg, bankU=bankU:
                            e.scalar_tensor_tensor(out=fuse[:, j, :], in0=bankU,
                                                   scalar=suc[:, j:j + 1], in1=gbuf[:, bg, :],
                                                   op0=ALU.mult, op1=ALU.mult),
                            (tum, tga, BAR2B if seg == 0 and j == 0 else None), "sV")
                psGU_free[2 + bg] = tfu
                gb_free[bg] = tfu
                fuse_toks.append(tfu)
            xn_free = fuse_toks[-1]

            for ob in range(NCD):
                dsl = ob % 2
                bankD = ps[4 + dsl]
                jg_mms = None
                for jg in range(8):
                    wdsl = jg % 4
                    twd = sc.do("sync",
                                lambda e, ob=ob, jg=jg, wdsl=wdsl:
                                e.dma_start(out=wdr[:, wdsl, :, :], in_=wdbf[ob, jg]),
                                [wd_free[wdsl], conv_tok[("d", ob // 4)]], "dS")

                    def dmm(e, jg=jg, wdsl=wdsl, bankD=bankD):
                        for jj in range(8):
                            i_ = e.matmul(bankD,
                                          wdr[:, wdsl, jj, :],
                                          fuse[:, jg * 8 + jj, :],
                                          start=(jg == 0 and jj == 0),
                                          stop=(jg == 7 and jj == 7))
                        return i_
                    deps = [twd] + (fuse_toks + [psD_free[dsl]] if jg == 0 else [])
                    tdm = sc.do("tensor", dmm, deps, "sT")
                    wd_free[wdsl] = tdm
                    jg_mms = tdm
                ksl = ob % 2
                tg_ = sc.do("scalar",
                            lambda e, ob=ob, ksl=ksl, bankD=bankD:
                            e.activation(out=gbuf[:, ksl, :], in_=bankD, func=AF.Silu,
                                         scale=sdc[:, ob:ob + 1]),
                            (jg_mms, gb_free[ksl]), "sA")
                psD_free[dsl] = tg_
                ssl4 = ob % 4
                tsl = sc.do("sync",
                            lambda e, seg=seg, ob=ob, ssl4=ssl4:
                            e.dma_start(out=sbufs[ssl4][:, :],
                                        in_=sscr[ob, :, seg * 512:(seg + 1) * 512]),
                            [s_free[ssl4], s_toks.get((ob, seg))], "dS")
                txq = sc.do("sync",
                            lambda e, seg=seg, ob=ob, ssl4=ssl4:
                            e.dma_start(out=xqb[ssl4][:, :],
                                        in_=dxqt[ob, :, seg * 512:(seg + 1) * 512]),
                            [xf3_free[ssl4]] + (BAR2B if seg == 0 and ob < 4 else []), "dS")
                t2 = sc.hz("vector",
                           lambda e, ksl=ksl, ssl4=ssl4:
                           e.tensor_mul(tmpg[:, ksl, :], gbuf[:, ksl, :], sbufs[ssl4][:, :]),
                           (tg_, t2_free[ksl], tsl))
                gb_free[ksl] = t2
                s_free[ssl4] = t2
                tout = sc.hz("vector",
                             lambda e, ksl=ksl, ssl4=ssl4:
                             e.tensor_add(outb[:, ksl, :], tmpg[:, ksl, :], xqb[ssl4][:, :]),
                             (t2, txq, ob_free[ksl]))
                t2_free[ksl] = tout
                xf3_free[ssl4] = tout
                tst = sc.do("scalar",
                            lambda e, seg=seg, ob=ob, ksl=ksl:
                            e.dma_start(out=dout[ob, :, seg * 512:(seg + 1) * 512],
                                        in_=outb[:, ksl, :]),
                            (tout,), "dA")
                ob_free[ksl] = tst

        FIN = dict(sc.cnt)
        sc.ops["sync"].append(([(s, FIN[s]) for s in DMA_SEM_NAMES if FIN[s] > 0],
                               None, None, 0))
        sc.check()

        def replay(name, e):
            for waits, fn, sig, inc in sc.ops[name]:
                for s, v in waits:
                    e.wait_ge(SEM[s], v)
                if fn is None:
                    continue
                inst = fn(e)
                if sig is not None:
                    inst.then_inc(SEM[sig], inc)

        with nc.Block() as block:
            @block.sync
            def _(e):
                replay("sync", e)

            @block.gpsimd
            def _(e):
                replay("gpsimd", e)

            @block.scalar
            def _(e):
                replay("scalar", e)

            @block.vector
            def _(e):
                replay("vector", e)

            @block.tensor
            def _(e):
                replay("tensor", e)

    return nc


_NC_CACHE = None


def _get_nc():
    global _NC_CACHE
    if _NC_CACHE is None:
        _NC_CACHE = build_program()
    return _NC_CACHE


def kernel(L0_attn_x, L0_attn_Wq, L0_attn_bq, L0_attn_Wk, L0_attn_bk,
           L0_attn_Wv, L0_attn_bv, L0_attn_Wo, L0_attn_bo,
           L0_ffn_w_gate, L0_ffn_w_up, L0_ffn_w_down,
           L0_ffn_s_gate, L0_ffn_s_up, L0_ffn_s_down):
    x = np.ascontiguousarray(np.asarray(L0_attn_x, dtype=np.float32))
    B = x.shape[0]
    asf = lambda a: np.ascontiguousarray(np.asarray(a, dtype=np.float32))
    wq = asf(L0_attn_Wq).reshape(NCD, P, D)
    wk = asf(L0_attn_Wk).reshape(NCD, P, D)
    wv = asf(L0_attn_Wv).reshape(NCD, P, D)
    wo = asf(L0_attn_Wo).reshape(NCD, P, D)
    bq = asf(L0_attn_bq).reshape(NCD, P)
    bk = asf(L0_attn_bk).reshape(NCD, P)
    bv = asf(L0_attn_bv).reshape(NCD, P)
    bo = asf(L0_attn_bo).reshape(NCD, P)
    wg8 = np.ascontiguousarray(
        np.asarray(L0_ffn_w_gate, dtype=np.int8).reshape(NCD, P, NJ, P).transpose(2, 1, 0, 3))
    wu8 = np.ascontiguousarray(
        np.asarray(L0_ffn_w_up, dtype=np.int8).reshape(NCD, P, NJ, P).transpose(2, 1, 0, 3))
    wd8 = np.ascontiguousarray(
        np.asarray(L0_ffn_w_down, dtype=np.int8)
        .reshape(8, 8, P, NCD, P).transpose(3, 0, 2, 1, 4).reshape(NCD, 8, P, 1024))
    sg = asf(L0_ffn_s_gate).reshape(NJ, P)
    su = asf(L0_ffn_s_up).reshape(NJ, P)
    sd = asf(L0_ffn_s_down).reshape(NCD, P)

    jj = np.arange(512)[None, :]
    pp = np.arange(P)[:, None]
    qmasks = []
    qmts = []
    for h in range(2):
        m = np.zeros((P, NQI * 512), dtype=np.float32)
        for qi in range(NQI):
            r = (2 * qi + h) % 4
            m[:, qi * 512:(qi + 1) * 512] = np.where(jj <= (pp + 128 * r), 0.0, -30000.0)
        qmasks.append(m.astype(ml_dtypes.bfloat16))
        kk = np.arange(P)[:, None]
        qq = np.arange(P)[None, :]
        mt = np.zeros((P, 256), dtype=np.float32)
        for s_off in range(2):
            vis = kk <= ((h - s_off) * 128 + qq)
            mt[:, s_off * 128:(s_off + 1) * 128] = np.where(vis, 0.0, -30000.0)
        qmts.append(mt.astype(ml_dtypes.bfloat16))

    nc = _get_nc()
    in_maps = []
    for core in range(8):
        b, h = divmod(core, 2)
        xb = np.ascontiguousarray(x[b])
        xqb = np.ascontiguousarray(x[b].reshape(32, P, D)[h::2].reshape(NQ, D))
        xqtb = np.ascontiguousarray(xqb.T.reshape(NCD, P, NQ))
        in_maps.append({
            "x": xb, "xq": xqb, "xqt": xqtb, "qmask": qmasks[h], "qmt": qmts[h],
            "wq": wq, "wk": wk, "wv": wv, "wo": wo,
            "bq": bq, "bk": bk, "bv": bv, "bo": bo,
            "wg8": wg8, "wu8": wu8, "wd8": wd8,
            "sg": sg, "su": su, "sd": sd,
        })
    res = run_bass_kernel_spmd(nc, in_maps, core_ids=list(range(8)))
    y = np.empty((B, S, D), dtype=np.float32)
    for core in range(8):
        b, h = divmod(core, 2)
        o = np.asarray(res.results[core]["out"], dtype=np.float32)
        o = o.reshape(D, NQ).T.reshape(NQI, P, D)
        y[b].reshape(32, P, D)[h::2] = o
    return y


if __name__ == "__main__":
    nc = build_program()
    n = sum(1 for _ in nc.all_instructions())
    print("instructions:", n)


# version 13
# speedup vs baseline: 1.0347x; 1.0222x over previous
import numpy as np
import ml_dtypes
from contextlib import ExitStack

import concourse.bass as bass
import concourse.mybir as mybir
from concourse.bass_utils import run_bass_kernel_spmd

F32 = mybir.dt.float32
BF16 = mybir.dt.bfloat16
I8 = mybir.dt.int8
AF = mybir.ActivationFunctionType
ALU = mybir.AluOpType
AX = mybir.AxisListType

P = 128
D = 2048
NCD = 16
NH = 16
S = 4096
NSEG_K = 8
NQ = 2048
NSEG_Q = 4
NQI = 16
NKC = [qi // 2 + 1 for qi in range(NQI)]
DI = 8192
NJ = 64
ISQ = 0.08838834764831845
EPS = 1e-5

ENGINES = ("sync", "gpsimd", "scalar", "vector", "tensor")
DMAQ = {"dS": ("sync", 6), "dA": ("scalar", 4), "dG": ("gpsimd", 3),
        "dW": ("gpsimd", 2)}
DMA_SEM_NAMES = [f"{q}{i}" for q, (_, n) in DMAQ.items() for i in range(n)]
ALL_SEMS = tuple(DMA_SEM_NAMES) + ("sT", "sV", "sA", "hV")
OWN = {"sync": {f"dS{i}" for i in range(6)},
       "gpsimd": {f"dG{i}" for i in range(3)} | {f"dW{i}" for i in range(2)},
       "scalar": {"sA"} | {f"dA{i}" for i in range(4)},
       "vector": {"sV"}, "tensor": {"sT"}}


def _flat(deps, out=None):
    if out is None:
        out = []
    for d in deps:
        if d is None:
            continue
        if isinstance(d, tuple) and len(d) == 2 and isinstance(d[0], str):
            out.append(d)
        else:
            _flat(d, out)
    return out


class Sched:
    def __init__(self):
        self.ops = {e: [] for e in ENGINES}
        self.cnt = {s: 0 for s in ALL_SEMS}
        self.maxwait = {s: 0 for s in ALL_SEMS}
        self.rot = {q: 0 for q in DMAQ}

    def do(self, eng, fn, deps=(), sig=None, hard=()):
        waits = {}
        for s, v in _flat(deps):
            if s in OWN[eng]:
                continue
            if waits.get(s, -1) < v:
                waits[s] = v
        for s, v in _flat(hard):
            if waits.get(s, -1) < v:
                waits[s] = v
        if sig in DMAQ:
            q = sig
            sig = f"{q}{self.rot[q] % DMAQ[q][1]}"
            self.rot[q] += 1
            prev = self.cnt[sig]
            if prev > 0 and waits.get(sig, -1) < prev:
                waits[sig] = prev
        for s, v in waits.items():
            if self.maxwait[s] < v:
                self.maxwait[s] = v
        tok = None
        inc = 0
        if sig is not None:
            inc = 16 if sig in DMA_SEM_NAMES else 1
            self.cnt[sig] += inc
            tok = (sig, self.cnt[sig])
        self.ops[eng].append((sorted(waits.items()), fn, sig, inc))
        return tok

    def hz(self, eng, fn, deps=()):
        assert eng == "vector"
        tok = self.do(eng, fn, deps, sig="hV")
        self.ops[eng].append(([("hV", tok[1])], None, None, 0))
        self.maxwait["hV"] = max(self.maxwait["hV"], tok[1])
        return tok

    def bar(self):
        return [(s, self.cnt[s]) for s in ALL_SEMS if not s.startswith("dW")]

    def check(self):
        for s in ALL_SEMS:
            assert self.maxwait[s] <= self.cnt[s], (s, self.maxwait[s], self.cnt[s])


def build_program(dbg=False):
    nc = bass.Bass("TRN2", target_bir_lowering=False)
    SCR = "ExternalOutput" if dbg else "Internal"

    dx = nc.dram_tensor("x", [S, D], F32, kind="ExternalInput").ap()
    dxq = nc.dram_tensor("xq", [NQ, D], F32, kind="ExternalInput").ap()
    dxqt = nc.dram_tensor("xqt", [NCD, P, NQ], F32, kind="ExternalInput").ap()
    dqm = nc.dram_tensor("qmask", [P, NQI * 512], BF16, kind="ExternalInput").ap()
    dqmt = nc.dram_tensor("qmt", [P, 256], BF16, kind="ExternalInput").ap()
    dwq = nc.dram_tensor("wq", [NCD, P, D], F32, kind="ExternalInput").ap()
    dwk = nc.dram_tensor("wk", [NCD, P, D], F32, kind="ExternalInput").ap()
    dwv = nc.dram_tensor("wv", [NCD, P, D], F32, kind="ExternalInput").ap()
    dwo = nc.dram_tensor("wo", [NCD, P, D], F32, kind="ExternalInput").ap()
    dbq = nc.dram_tensor("bq", [NCD, P], F32, kind="ExternalInput").ap()
    dbk = nc.dram_tensor("bk", [NCD, P], F32, kind="ExternalInput").ap()
    dbv = nc.dram_tensor("bv", [NCD, P], F32, kind="ExternalInput").ap()
    dbo = nc.dram_tensor("bo", [NCD, P], F32, kind="ExternalInput").ap()
    dwg = nc.dram_tensor("wg8", [NJ, P, NCD, P], I8, kind="ExternalInput").ap()
    dwu = nc.dram_tensor("wu8", [NJ, P, NCD, P], I8, kind="ExternalInput").ap()
    dwd = nc.dram_tensor("wd8", [NCD, 8, P, 1024], I8, kind="ExternalInput").ap()
    dsg = nc.dram_tensor("sg", [NJ, P], F32, kind="ExternalInput").ap()
    dsu = nc.dram_tensor("su", [NJ, P], F32, kind="ExternalInput").ap()
    dsd = nc.dram_tensor("sd", [NCD, P], F32, kind="ExternalInput").ap()
    dout = nc.dram_tensor("out", [NCD, P, NQ], F32, kind="ExternalOutput").ap()

    xnscr = nc.dram_tensor("xnscr", [NCD, P, S], BF16, kind=SCR).ap()
    xnqscr = nc.dram_tensor("xnqscr", [NCD, P, NQ], BF16, kind=SCR).ap()
    kscr = nc.dram_tensor("kscr", [NH, P, S], BF16, kind=SCR).ap()
    vscr = nc.dram_tensor("vscr", [S, D], BF16, kind=SCR).ap()
    qscr = nc.dram_tensor("qscr", [NH, P, NQ], BF16, kind=SCR).ap()
    sscr = nc.dram_tensor("sscr", [NQI, P, D], BF16, kind=SCR).ap()
    fusescr = nc.dram_tensor("fusescr", [NSEG_Q, P, NJ * 512], BF16, kind=SCR).ap()
    dwnscr = nc.dram_tensor("dwnscr", [NSEG_Q, 2, 2, 4, P, 512], F32, kind=SCR).ap()
    wgbf = nc.dram_tensor("wgbf", [NJ, P, D], BF16, kind="Internal").ap()
    wubf = nc.dram_tensor("wubf", [NJ, P, D], BF16, kind="Internal").ap()
    wdbf = nc.dram_tensor("wdbf", [NCD, 8, P, 1024], BF16, kind="Internal").ap()

    with ExitStack() as st:
        en = st.enter_context
        idb = en(nc.sbuf_tensor("idb", [P, P], BF16))
        id32 = en(nc.sbuf_tensor("id32", [P, P], F32))
        qmb = en(nc.sbuf_tensor("qmb", [P, NQI * 512], BF16))
        konst = en(nc.sbuf_tensor("konst", [P, 4], F32))
        bqs = en(nc.sbuf_tensor("bqs", [P, NCD], F32))
        bks = en(nc.sbuf_tensor("bks", [P, NCD], F32))
        bvs = en(nc.sbuf_tensor("bvs", [P, NCD], F32))
        sgc = en(nc.sbuf_tensor("sgc", [P, NJ], F32))
        suc = en(nc.sbuf_tensor("suc", [P, NJ], F32))
        boc = en(nc.sbuf_tensor("boc", [P, NCD], F32))
        sdc = en(nc.sbuf_tensor("sdc", [P, NCD], F32))
        xf = en(nc.sbuf_tensor("xf", [P, 2, 2, D], F32))
        outb = en(nc.sbuf_tensor("outb", [P, 2, 512], F32))
        tmpg = en(nc.sbuf_tensor("tmpg", [P, 2, 512], F32))
        stats = en(nc.sbuf_tensor("stats", [P, 64], F32))
        lnbuf = en(nc.sbuf_tensor("lnbuf", [P, 8192], BF16))
        arA = en(nc.sbuf_tensor("arA", [P, 32768], BF16))
        arC = en(nc.sbuf_tensor("arC", [P, 16384], BF16))
        qhb = en(nc.sbuf_tensor("qhb", [P, 4096], BF16))
        pstg = en(nc.sbuf_tensor("pstg", [P, 2, 1024], BF16))
        qmtb = en(nc.sbuf_tensor("qmtb", [P, 256], BF16))
        onesb = en(nc.sbuf_tensor("onesb", [1, P], BF16))
        onek = en(nc.sbuf_tensor("onek", [P, 1], BF16))
        nonesb = en(nc.sbuf_tensor("nonesb", [1, P], BF16))
        ptb = en(nc.sbuf_tensor("ptb", [P, 2, 512], BF16))
        negb = en(nc.sbuf_tensor("negb", [P, 2], F32))
        bndbf = en(nc.sbuf_tensor("bndbf", [1, 2], BF16))
        pon = en(nc.sbuf_tensor("pon", [P, 2, P], BF16))
        gbuf = en(nc.sbuf_tensor("gbuf", [P, 2, 512], BF16))

        psA = en(nc.psum_tensor("psA", [P, 1024], F32))
        psE = en(nc.psum_tensor("psE", [P, 1024], F32))
        psO0 = en(nc.psum_tensor("psO0", [P, 512], F32))
        psO1 = en(nc.psum_tensor("psO1", [P, 512], F32))
        psB = en(nc.psum_tensor("psB", [P, 1024], BF16))
        psB2 = en(nc.psum_tensor("psB2", [P, 1024], BF16))
        ps = [psA[:, 0:512], psA[:, 512:1024], psE[:, 0:512], psE[:, 512:1024],
              psO0[:, 0:512], psO1[:, 0:512]]
        psOP = [psO0, psO1]

        vr_bo = xf[0:16, 1, 0, 0:128]
        vr_sd = xf[0:16, 1, 0, 128:256]
        vr_bq = xf[0:16, 1, 1, 0:128]
        vr_bk = xf[0:16, 1, 1, 128:256]
        vr_bv = xf[0:16, 1, 1, 256:384]
        vr_sg = xf[0:64, 0, 0, 0:128]
        vr_su = xf[0:64, 0, 0, 128:256]
        sqscr = qhb[:, 0:2048]

        SEM = {s: en(nc.semaphore(s)) for s in ALL_SEMS}
        sc = Sched()

        TRB = [(psB, psB[:, 0:512]), (psB2, psB2[:, 0:512])]

        sc.do("sync", lambda e: e.dma_start(out=qmb[:], in_=dqm[:, :]), (), "dS")
        sc.do("sync", lambda e: e.dma_start(out=qmtb[:], in_=dqmt[:, :]), (), "dS")
        sc.do("sync", lambda e: e.dma_start(out=vr_bq, in_=dbq[:, :]), (), "dS")
        sc.do("sync", lambda e: e.dma_start(out=vr_bk, in_=dbk[:, :]), (), "dS")
        sc.do("sync", lambda e: e.dma_start(out=vr_bv, in_=dbv[:, :]), (), "dS")
        sc.do("sync", lambda e: e.dma_start(out=vr_sg, in_=dsg[:, :]), (), "dS")
        sc.do("sync", lambda e: e.dma_start(out=vr_su, in_=dsu[:, :]), (), "dS")
        sc.do("sync", lambda e: e.dma_start(out=vr_bo, in_=dbo[:, :]), (), "dS")
        t_ld0 = sc.do("sync", lambda e: e.dma_start(out=vr_sd, in_=dsd[:, :]), (), "dS")

        def _setup_g(e):
            e.memset(idb[:], 1.0)
            e.affine_select(idb[:], idb[:], [[-1, P]], ALU.is_equal, 0.0,
                            base=0, channel_multiplier=1)
            e.memset(id32[:], 1.0)
            e.affine_select(id32[:], id32[:], [[-1, P]], ALU.is_equal, 0.0,
                            base=0, channel_multiplier=1)
            e.memset(onesb[:], 1.0)
            e.memset(onek[:], 1.0)
            e.memset(nonesb[:], -1.0)
            return e.memset(konst[:, 0:1], EPS)
        t_g0 = sc.do("gpsimd", _setup_g, (), "sV")

        conv_tok = {}
        for nm, (src, dst, fl) in (("g", (dwg, wgbf, True)), ("u", (dwu, wubf, True)),
                                   ("d", (dwd, wdbf, False))):
            for q4 in range(4):
                a, b_ = q4 * 4, (q4 + 1) * 4
                if fl:
                    a, b_ = q4 * (NJ // 4), (q4 + 1) * (NJ // 4)
                    fn = (lambda e, s_=src, d_=dst, a=a, b=b_:
                          e.dma_start(out=d_[a:b], in_=s_[a:b].rearrange("a p c m -> a p (c m)")))
                else:
                    fn = (lambda e, s_=src, d_=dst, a=a, b=b_:
                          e.dma_start(out=d_[a:b], in_=s_[a:b].rearrange("a g p m -> a g p m")))
                conv_tok[(nm, q4)] = sc.do("gpsimd", fn, (), "dW")

        t_tr = sc.do("tensor", lambda e: e.transpose(ps[0][:, 0:16], vr_bq, id32[0:16, 0:16]), (t_ld0, t_g0), "sT")
        t_tr = sc.do("tensor", lambda e: e.transpose(ps[0][:, 16:32], vr_bk, id32[0:16, 0:16]), (), "sT")
        t_tr = sc.do("tensor", lambda e: e.transpose(ps[0][:, 32:48], vr_bv, id32[0:16, 0:16]), (), "sT")
        t_tr = sc.do("tensor", lambda e: e.transpose(ps[0][:, 64:128], vr_sg, id32[0:64, 0:64]), (), "sT")
        t_tr = sc.do("tensor", lambda e: e.transpose(ps[0][:, 128:192], vr_su, id32[0:64, 0:64]), (), "sT")
        t_tr = sc.do("tensor", lambda e: e.transpose(ps[0][:, 192:208], vr_bo, id32[0:16, 0:16]), (), "sT")
        t_tr = sc.do("tensor", lambda e: e.transpose(ps[0][:, 208:224], vr_sd, id32[0:16, 0:16]), (), "sT")
        sc.do("scalar", lambda e: e.activation(out=bqs[:, :], in_=ps[0][:, 0:16], func=AF.Copy, scale=ISQ), (t_tr,), "sA")
        sc.do("scalar", lambda e: e.copy(bks[:, :], ps[0][:, 16:32]), (), "sA")
        sc.do("scalar", lambda e: e.copy(bvs[:, :], ps[0][:, 32:48]), (), "sA")
        sc.do("scalar", lambda e: e.copy(sgc[:, :], ps[0][:, 64:128]), (), "sA")
        sc.do("scalar", lambda e: e.copy(suc[:, :], ps[0][:, 128:192]), (), "sA")
        sc.do("scalar", lambda e: e.copy(boc[:, :], ps[0][:, 192:208]), (), "sA")
        sc.do("scalar", lambda e: e.copy(sdc[:, :], ps[0][:, 208:224]), (), "sA")

        BAR0 = sc.bar()

        xnstage = [arA[:, 0:8192].rearrange("p (c t) -> p c t", t=512),
                   arA[:, 8192:16384].rearrange("p (c t) -> p c t", t=512)]
        xntm = [lnbuf[:, 0:2048], lnbuf[:, 2048:4096]]
        grp_i = [0]

        def emit_ln(src_ap, dst_scr, ntok, bar):
            nhs = ntok // 256
            xf_free = [None, None]
            stage_free = [None, None]
            trb_free = [None, None]
            hist = {}
            store_toks = []
            for hsg in range(nhs):
                sl = hsg % 2
                seg = hsg // 2
                ssl = seg % 2
                tld = sc.do("sync",
                            lambda e, sl=sl, o=hsg * 256, src_ap=src_ap:
                            e.dma_start(out=xf[:, sl, :, :],
                                        in_=src_ap[o:o + 256, :].rearrange("(n p) d -> p n d", p=P)),
                            [xf_free[sl]] + (bar if hsg < 2 else []), "dS")
                xf_readers = []
                for i in range(2):
                    tt = hsg * 2 + i
                    c0 = (tt % 4) * 8
                    xin = xf[:, sl, i, :]
                    old = hist.get(tt - 4, {})
                    tsq = sc.do("scalar",
                                lambda e, xin=xin, c0=c0:
                                e.activation(out=sqscr[:, :], in_=xin, func=AF.Square,
                                             accum_out=stats[:, c0:c0 + 1]),
                                (tld, old.get("var")) + (tuple(bar) if tt < 2 else ()), "sA")
                    tns = sc.do("vector",
                                lambda e, xin=xin, c0=c0:
                                e.reduce_sum(stats[:, c0 + 1:c0 + 2], xin, axis=AX.X, negate=True),
                                (tld, old.get("nm"), old.get("ms")), "sV")
                    tnm = sc.do("scalar",
                                lambda e, c0=c0:
                                e.activation(out=stats[:, c0 + 2:c0 + 3], in_=stats[:, c0 + 1:c0 + 2],
                                             func=AF.Copy, scale=1.0 / D),
                                (tns, old.get("xn")), "sA")
                    tms = sc.do("scalar",
                                lambda e, c0=c0:
                                e.activation(out=stats[:, c0 + 3:c0 + 4], in_=stats[:, c0 + 1:c0 + 2],
                                             func=AF.Square, scale=1.0 / D),
                                (tns, old.get("var")), "sA")
                    tvar = sc.do("vector",
                                 lambda e, c0=c0:
                                 e.scalar_tensor_tensor(out=stats[:, c0 + 4:c0 + 5],
                                                        in0=stats[:, c0:c0 + 1], scalar=1.0 / D,
                                                        in1=stats[:, c0 + 3:c0 + 4],
                                                        op0=ALU.mult, op1=ALU.subtract),
                                 (tsq, tms, old.get("std")), "sV")
                    tstd = sc.do("scalar",
                                 lambda e, c0=c0:
                                 e.activation(out=stats[:, c0 + 5:c0 + 6], in_=stats[:, c0 + 4:c0 + 5],
                                              func=AF.Sqrt, bias=konst[:, 0:1]),
                                 (tvar, old.get("ri")), "sA")
                    tri = sc.hz("vector",
                                lambda e, c0=c0:
                                e.reciprocal(stats[:, c0 + 6:c0 + 7], stats[:, c0 + 5:c0 + 6]),
                                (tstd,))
                    xsl = tt % 2
                    txn = sc.do("vector",
                                lambda e, xin=xin, c0=c0, xsl=xsl:
                                e.tensor_scalar(out=xntm[xsl][:, :], in0=xin,
                                                scalar1=stats[:, c0 + 2:c0 + 3],
                                                scalar2=stats[:, c0 + 6:c0 + 7],
                                                op0=ALU.add, op1=ALU.mult),
                                (tri, tnm, hist.get(tt - 2, {}).get("tr_last")), "sV")
                    xf_readers += [tsq, tns, txn]
                    evs = []
                    trl = None
                    for g in range(4):
                        gsl = grp_i[0] % 2
                        grp_i[0] += 1
                        bank, bslice = TRB[gsl]

                        def trfn(e, xsl=xsl, g=g, bslice=bslice):
                            for cc in range(4):
                                c = g * 4 + cc
                                i_ = e.transpose(bslice[:, cc * 128:(cc + 1) * 128],
                                                 xntm[xsl][:, c * 128:(c + 1) * 128], idb[:])
                            return i_
                        ttr = sc.do("tensor", trfn, (txn, trb_free[gsl], t_g0), "sT")
                        trl = ttr
                        tev = sc.do("vector",
                                    lambda e, ssl=ssl, tt=tt, g=g, bslice=bslice:
                                    e.tensor_copy(xnstage[ssl][:, g * 4:(g + 1) * 4, (tt % 4) * 128:(tt % 4 + 1) * 128],
                                                  bslice.rearrange("p (c t) -> p c t", t=P)),
                                    (ttr, stage_free[ssl]), "sV")
                        trb_free[gsl] = tev
                        evs.append(tev)
                    hist[tt] = {"ns": tns, "nm": tnm, "ms": tms, "var": tvar,
                                "std": tstd, "ri": tri, "xn": txn,
                                "tr_last": trl, "evs": evs}
                xf_free[sl] = xf_readers
                if hsg % 2 == 1:
                    all_evs = (hist[hsg * 2 - 2]["evs"] + hist[hsg * 2 - 1]["evs"] +
                               hist[hsg * 2]["evs"] + hist[hsg * 2 + 1]["evs"])
                    tst = sc.do("scalar",
                                lambda e, seg=seg, ssl=ssl, dst_scr=dst_scr:
                                e.dma_start(out=dst_scr[:, :, seg * 512:(seg + 1) * 512].rearrange("c p t -> p c t"),
                                            in_=xnstage[ssl][:, :, :]),
                                all_evs, "dA")
                    stage_free[ssl] = tst
                    store_toks.append(tst)
            return store_toks

        emit_ln(dx, xnscr, S, BAR0)
        emit_ln(dxq, xnqscr, NQ, sc.bar())
        BAR1a = sc.bar()

        wres = arC[:, 0:16384].rearrange("p (c m) -> p c m", m=1024)
        xnb = [arA[:, 16384:24576].rearrange("p (c t) -> p c t", t=512),
               arA[:, 24576:32768].rearrange("p (c t) -> p c t", t=512)]
        kstage = [arA[:, 0:4096].rearrange("p (h t) -> p h t", t=512),
                  arA[:, 4096:8192].rearrange("p (h t) -> p h t", t=512)]
        vstage = [arA[:, 0:1024], arA[:, 1024:2048]]

        def emit_kq(wsrc, biascol, scale, src_scr, ntokens, dst_scr, bar):
            nseg = ntokens // 512
            wres_free = bar
            xnb_free = [None, None]
            bank_free = [None, None]
            kst_free = [None, None]
            last_toks = []
            for p_ in range(2):
                tw = sc.do("gpsimd",
                           lambda e, p_=p_, wsrc=wsrc:
                           e.dma_start(out=wres[:, :, :],
                                       in_=wsrc[:, :, p_ * 1024:(p_ + 1) * 1024].rearrange("c p m -> p c m")),
                           [wres_free] + (bar if p_ == 0 else []), "dG")
                for seg in range(nseg):
                    sl = seg % 2
                    tx = sc.do("sync",
                               lambda e, seg=seg, sl=sl, src_scr=src_scr:
                               e.dma_start(out=xnb[sl][:, :, :],
                                           in_=src_scr[:, :, seg * 512:(seg + 1) * 512].rearrange("c p t -> p c t")),
                               [xnb_free[sl]] + (bar if p_ == 0 and seg < 2 else []), "dS")
                    last_ev = None
                    for hh in range(8):
                        bsl = hh % 2
                        bank = ps[bsl]

                        def mmfn(e, sl=sl, hh=hh, bank=bank):
                            for c in range(NCD):
                                i_ = e.matmul(bank[:, 0:512],
                                              wres[:, c, hh * 128:(hh + 1) * 128],
                                              xnb[sl][:, c, :],
                                              start=(c == 0), stop=(c == NCD - 1))
                            return i_
                        tmm = sc.do("tensor", mmfn, (tw, tx, bank_free[bsl]), "sT")
                        gh = p_ * 8 + hh
                        tev = sc.do("scalar",
                                    lambda e, sl=sl, hh=hh, bank=bank, gh=gh, biascol=biascol, scale=scale:
                                    e.activation(out=kstage[sl][:, hh, :], in_=bank[:, 0:512],
                                                 func=AF.Identity, bias=biascol[:, gh:gh + 1], scale=scale),
                                    (tmm, kst_free[sl]), "sA")
                        bank_free[bsl] = tev
                        last_ev = tev
                    tst = sc.do("scalar",
                                lambda e, p_=p_, seg=seg, sl=sl, dst_scr=dst_scr:
                                e.dma_start(out=dst_scr[p_ * 8:(p_ + 1) * 8, :, seg * 512:(seg + 1) * 512].rearrange("h p t -> p h t"),
                                            in_=kstage[sl][:, :, :]),
                                (), "dA", hard=(last_ev,))
                    kst_free[sl] = tst
                    xnb_free[sl] = tst
                    last_toks.append(tst)
                wres_free = last_toks[-1]
            return last_toks

        def emit_v(bar):
            wres_free = bar
            xnb_free = [None, None]
            bank_free = [None, None]
            vst_free = [None, None]
            last_toks = []
            for p_ in range(2):
                tw = sc.do("gpsimd",
                           lambda e, p_=p_:
                           e.dma_start(out=wres[:, :, :],
                                       in_=dwv[:, :, p_ * 1024:(p_ + 1) * 1024].rearrange("c p m -> p c m")),
                           [wres_free] + (bar if p_ == 0 else []), "dG")
                for seg in range(NSEG_K):
                    sl = seg % 2
                    tx = sc.do("sync",
                               lambda e, seg=seg, sl=sl:
                               e.dma_start(out=xnb[sl][:, :, :],
                                           in_=xnscr[:, :, seg * 512:(seg + 1) * 512].rearrange("c p t -> p c t")),
                               [xnb_free[sl]] + (bar if p_ == 0 and seg < 2 else []), "dS")
                    seg_toks = []
                    for sub in range(4):
                        vsl = sub % 2
                        evs = []
                        for dt in range(2):
                            bank = ps[2 + dt]

                            def mmfn(e, sl=sl, sub=sub, dt=dt, bank=bank):
                                for c in range(NCD):
                                    i_ = e.matmul(bank[:, 0:512],
                                                  xnb[sl][:, c, sub * 128:(sub + 1) * 128],
                                                  wres[:, c, dt * 512:(dt + 1) * 512],
                                                  start=(c == 0), stop=(c == NCD - 1))
                                return i_
                            tmm = sc.do("tensor", mmfn, (tw, tx, bank_free[dt]), "sT")
                            tev = sc.do("vector",
                                        lambda e, vsl=vsl, dt=dt, bank=bank:
                                        e.tensor_copy(vstage[vsl][:, dt * 512:(dt + 1) * 512], bank[:, 0:512]),
                                        (tmm, vst_free[vsl]), "sV")
                            bank_free[dt] = tev
                            evs.append(tev)
                        tstv = sc.do("scalar",
                                     lambda e, p_=p_, seg=seg, sub=sub, vsl=vsl:
                                     e.dma_start(out=vscr[seg * 512 + sub * 128: seg * 512 + (sub + 1) * 128,
                                                          p_ * 1024:(p_ + 1) * 1024],
                                                 in_=vstage[vsl][:, :]),
                                     evs, "dA")
                        vst_free[vsl] = tstv
                        seg_toks.append(tstv)
                    xnb_free[sl] = seg_toks
                    last_toks += seg_toks
                wres_free = last_toks[-1]
            return last_toks

        emit_kq(dwk, bks, 1.0, xnscr, S, kscr, BAR1a)
        emit_v(sc.bar())
        emit_kq(dwq, bqs, ISQ, xnqscr, NQ, qscr, sc.bar())
        BAR1 = sc.bar()

        kbuf = [arC[:, 0:4096], arC[:, 4096:8192]]
        vbuf = [arC[:, 8192:12288].rearrange("p (n d) -> p n d", d=P),
                arC[:, 12288:16384].rearrange("p (n d) -> p n d", d=P)]
        qbuf = [qhb[:, 0:2048], qhb[:, 2048:4096]]
        attn = arA
        SS = [psA[:, 0:512], psA[:, 512:1024], psE[:, 0:512]]
        MISC = psE[:, 512:1024]
        ksq = [lnbuf[:, 0:4096], lnbuf[:, 4096:8192]]

        kq_free = [None, None]
        v_free = [None, None]
        ss_free = [None, None, None]
        pstf2 = pstg.rearrange("p a t -> p (a t)")
        pstg4 = [pstf2[:, i * 512:(i + 1) * 512] for i in range(4)]
        pst_free = [None, None, None, None]
        ptb_free = [None, None]
        psO_free = [None, None]
        trb_free = [None, None]
        pon_free = [None, None]
        lcol_free = [None, None]
        misc_tok = [None]
        bound_tok = [None, None]
        gm = [0]
        sm = [0]
        pm = [0]
        ptm = [0]
        LBASE = {0: 0, 1: 16}
        BQ = 32

        def emit_bound_steps(h2, hl2, tk2, tq2):
            kv = ksq[hl2]
            steps = []

            def chain(eng, fn, extra=(), sig=None, hz=False):
                def run(fn=fn, eng=eng, extra=extra, sig=sig, hz=hz):
                    if hz:
                        t = sc.hz(eng, fn, (misc_tok[0],) + extra)
                    else:
                        t = sc.do(eng, fn, (misc_tok[0],) + extra,
                                  sig if sig else {"scalar": "sA", "vector": "sV",
                                                   "tensor": "sT"}[eng])
                    misc_tok[0] = t
                    return t
                steps.append(run)

            chain("scalar", lambda e, kv=kv, hl2=hl2:
                  e.activation(out=kv[:, :], in_=kbuf[hl2][:, :], func=AF.Square),
                  (tk2,))
            for i in range(8):
                chain("tensor", lambda e, kv=kv, i=i:
                      e.matmul(MISC[0:1, 0:512], onek[:, 0:1],
                               kv[:, i * 512:(i + 1) * 512], start=True, stop=True))
                chain("vector", lambda e, i=i:
                      e.reduce_max(stats[0:1, BQ + i:BQ + i + 1],
                                   MISC[0:1, 0:512], axis=AX.X), hz=True)
            chain("vector", lambda e:
                  e.reduce_max(stats[0:1, BQ + 8:BQ + 9],
                               stats[0:1, BQ:BQ + 8], axis=AX.X), hz=True)
            chain("scalar", lambda e, kv=kv, hl2=hl2:
                  e.activation(out=kv[:, 0:2048], in_=qbuf[hl2][:, :], func=AF.Square),
                  (tq2,))
            for i in range(4):
                chain("tensor", lambda e, kv=kv, i=i:
                      e.matmul(MISC[0:1, 0:512], onek[:, 0:1],
                               kv[:, i * 512:(i + 1) * 512], start=True, stop=True))
                chain("vector", lambda e, i=i:
                      e.reduce_max(stats[0:1, BQ + 9 + i:BQ + 10 + i],
                                   MISC[0:1, 0:512], axis=AX.X), hz=True)
            chain("vector", lambda e:
                  e.reduce_max(stats[0:1, BQ + 13:BQ + 14],
                               stats[0:1, BQ + 9:BQ + 13], axis=AX.X), hz=True)
            chain("vector", lambda e:
                  e.tensor_mul(stats[0:1, BQ + 14:BQ + 15],
                               stats[0:1, BQ + 8:BQ + 9],
                               stats[0:1, BQ + 13:BQ + 14]), hz=True)
            chain("scalar", lambda e, hl2=hl2:
                  e.activation(out=bndbf[0:1, hl2:hl2 + 1],
                               in_=stats[0:1, BQ + 14:BQ + 15], func=AF.Sqrt))
            chain("tensor", lambda e, hl2=hl2:
                  e.matmul(MISC[:, 288:289], nonesb[0:1, 0:P],
                           bndbf[0:1, hl2:hl2 + 1], start=True, stop=True))

            def fin(hl2=hl2):
                t = sc.do("scalar", lambda e, hl2=hl2:
                          e.copy(negb[:, hl2:hl2 + 1], MISC[:, 288:289]),
                          (misc_tok[0],), "sA")
                misc_tok[0] = t
                bound_tok[hl2] = t
            steps.append(fin)
            return steps

        ld_toks = {}

        def emit_loads(h):
            hl = h % 2
            dep_ld = [kq_free[hl]] + (BAR1 if h < 2 else [])
            tk = sc.do("sync",
                       lambda e, h=h, hl=hl: e.dma_start(out=kbuf[hl][:, :], in_=kscr[h]),
                       dep_ld, "dS")
            tq = sc.do("sync",
                       lambda e, h=h, hl=hl: e.dma_start(out=qbuf[hl][:, :], in_=qscr[h]),
                       dep_ld, "dS")
            tvl = sc.do("sync",
                        lambda e, h=h, hl=hl:
                        e.dma_start(out=vbuf[hl][:, :, :],
                                    in_=vscr.rearrange("(n p) d -> p n d", p=P)[:, :, h * 128:(h + 1) * 128]),
                        [v_free[hl]] + (BAR1 if h < 2 else []), "dS")
            ld_toks[h] = (tk, tq, tvl)

        emit_loads(0)
        bsteps = emit_bound_steps(0, 0, ld_toks[0][0], ld_toks[0][1])
        for s_ in bsteps:
            s_()
        for h in range(NH):
            hl = h % 2
            tk, tq, tvl = ld_toks[h]
            bsteps = []
            if h + 1 < NH:
                emit_loads(h + 1)
                bsteps = emit_bound_steps(h + 1, 1 - hl,
                                          ld_toks[h + 1][0], ld_toks[h + 1][1])
            kq_readers = []
            v_readers = []
            texp_last = {}
            pend = [None]

            def p2_tail(qi, pv_last):
                par = qi % 2
                lb = LBASE[par]
                tls = sc.hz("vector",
                            lambda e, qi=qi, lb=lb:
                            e.reduce_sum(stats[:, lb + 8:lb + 9],
                                         stats[:, lb:lb + NKC[qi]], axis=AX.X),
                            (texp_last[qi],))
                lcol_free[par] = tls
                tli = sc.hz("vector",
                            lambda e, lb=lb:
                            e.reciprocal(stats[:, lb + 9:lb + 10],
                                         stats[:, lb + 8:lb + 9]),
                            (tls,))
                tno = sc.do("vector",
                            lambda e, par=par, lb=lb:
                            e.tensor_scalar_mul(pon[:, par, :], psOP[par][:, 0:128],
                                                stats[:, lb + 9:lb + 10]),
                            (tli, pv_last, pon_free[par]), "sV")
                psO_free[par] = tno
                gsl = gm[0] % 2
                gm[0] += 1
                bankT, bslice = TRB[gsl]
                tot = sc.do("tensor",
                            lambda e, par=par, bslice=bslice:
                            e.transpose(bslice[:, 0:128], pon[:, par, :], idb[:]),
                            (tno, trb_free[gsl]), "sT")
                tfin = sc.do("scalar",
                             lambda e, h=h, qi=qi, bslice=bslice:
                             e.activation(out=attn[:, h * 2048 + qi * 128: h * 2048 + (qi + 1) * 128],
                                          in_=bslice[:, 0:128],
                                          func=AF.Identity, bias=bvs[:, h:h + 1]),
                             (tot,), "sA")
                trb_free[gsl] = tfin
                pon_free[par] = tot

            def s_chunk(qi, kc):
                par = qi % 2
                nkc = NKC[qi]
                ssl = sm[0] % 3
                sm[0] += 1
                bank = SS[ssl]
                tmm = sc.do("tensor",
                            lambda e, qi=qi, kc=kc, hl=hl, bank=bank:
                            e.matmul(bank, qbuf[hl][:, qi * 128:(qi + 1) * 128],
                                     kbuf[hl][:, kc * 512:(kc + 1) * 512],
                                     start=True, stop=True),
                            (tk, tq, ss_free[ssl]), "sT")
                kq_readers.append(tmm)
                dep = tmm
                if kc == nkc - 1:
                    dep = sc.hz("vector",
                                lambda e, qi=qi, bank=bank:
                                e.tensor_add(bank, bank, qmb[:, qi * 512:(qi + 1) * 512]),
                                (tmm,))
                psl = pm[0] % 4
                pm[0] += 1
                texp = sc.do("scalar",
                             lambda e, psl=psl, bank=bank, hl=hl, par=par, kc=kc:
                             e.activation(out=pstg4[psl][:, 0:512], in_=bank,
                                          func=AF.Exp, bias=negb[:, hl:hl + 1],
                                          accum_out=stats[:, LBASE[par] + kc:LBASE[par] + kc + 1]),
                             (dep, pst_free[psl], bound_tok[hl]) +
                             ((lcol_free[par],) if kc == 0 else ()), "sA")
                ss_free[ssl] = texp
                texp_last[qi] = texp
                return (qi, kc, psl, texp)

            def tp_pv(item):
                qi, kc, psl, texp = item
                par = qi % 2
                nkc = NKC[qi]
                gsl = gm[0] % 2
                gm[0] += 1
                bankT, bslice = TRB[gsl]

                def trfn(e, psl=psl, bslice=bslice):
                    for ks in range(4):
                        i_ = e.transpose(bslice[:, ks * 128:(ks + 1) * 128],
                                         pstg4[psl][:, ks * 128:(ks + 1) * 128], idb[:])
                    return i_
                ttr = sc.do("tensor", trfn, (texp, trb_free[gsl]), "sT")
                pst_free[psl] = ttr
                ptl = ptm[0] % 2
                ptm[0] += 1
                tev = sc.do("vector",
                            lambda e, ptl=ptl, bslice=bslice:
                            e.tensor_copy(ptb[:, ptl, :], bslice),
                            (ttr, ptb_free[ptl]), "sV")
                trb_free[gsl] = tev

                def pvfn(e, qi=qi, kc=kc, nkc=nkc, ptl=ptl, hl=hl, par=par):
                    for ks in range(4):
                        i_ = e.matmul(psOP[par][:, 0:128],
                                      ptb[:, ptl, ks * 128:(ks + 1) * 128],
                                      vbuf[hl][:, kc * 4 + ks, :],
                                      start=(kc == 0 and ks == 0),
                                      stop=(kc == nkc - 1 and ks == 3))
                    return i_
                tpv = sc.do("tensor", pvfn,
                            (tev, tvl) + ((psO_free[par],) if kc == 0 else ()), "sT")
                ptb_free[ptl] = tpv
                v_readers.append(tpv)
                if kc == nkc - 1:
                    p2_tail(qi, tpv)

            for qi in range(NQI):
                for kc in range(NKC[qi]):
                    item = s_chunk(qi, kc)
                    if pend[0] is not None:
                        tp_pv(pend[0])
                    pend[0] = item
                if 1 <= qi < 13:
                    for _ in range(3):
                        if bsteps:
                            bsteps.pop(0)()
            if pend[0] is not None:
                tp_pv(pend[0])
                pend[0] = None
            while bsteps:
                bsteps.pop(0)()
            kq_free[hl] = kq_readers
            v_free[hl] = v_readers
        BAR2 = sc.bar()

        sstg2 = [pstg.rearrange("p a t -> p (a t)")[:, i * 512:(i + 1) * 512] for i in range(4)]
        wres_free = BAR2
        bank_free = [None, None]
        sst_free = [None, None, None, None]
        s_toks = {}
        for p_ in range(2):
            tw = sc.do("gpsimd",
                       lambda e, p_=p_:
                       e.dma_start(out=wres[:, :, :],
                                   in_=dwo[:, :, p_ * 1024:(p_ + 1) * 1024].rearrange("c p m -> p c m")),
                       [wres_free] + (BAR2 if p_ == 0 else []), "dG")
            for g_ in range(4):
                for obl in range(8):
                    obg = p_ * 8 + obl
                    bsl = obl % 2
                    bank = ps[bsl]
                    ssl = (g_ * 8 + obl) % 4

                    def mmfn(e, g_=g_, obl=obl, bank=bank):
                        for c in range(NCD):
                            i_ = e.matmul(bank,
                                          wres[:, c, obl * 128:(obl + 1) * 128],
                                          attn[:, c * 2048 + g_ * 512: c * 2048 + (g_ + 1) * 512],
                                          start=(c == 0), stop=(c == NCD - 1))
                        return i_
                    tmm = sc.do("tensor", mmfn,
                                [tw, bank_free[bsl]] + (BAR2 if p_ == 0 and g_ == 0 and obl == 0 else []), "sT")
                    tev = sc.do("scalar",
                                lambda e, obg=obg, ssl=ssl, bank=bank:
                                e.activation(out=sstg2[ssl][:, :], in_=bank,
                                             func=AF.Identity, bias=boc[:, obg:obg + 1]),
                                (tmm, sst_free[ssl]), "sA")
                    bank_free[bsl] = tev
                    tst = sc.do("scalar",
                                lambda e, obg=obg, g_=g_, ssl=ssl:
                                e.dma_start(out=sscr[obg, :, g_ * 512:(g_ + 1) * 512],
                                            in_=sstg2[ssl][:, :]),
                                (), "dA", hard=(tev,))
                    sst_free[ssl] = tst
                    s_toks[(obg, g_)] = tst
            wres_free = tst
        BAR2B = sc.bar()

        fuse = arA.rearrange("p (j t) -> p j t", t=512)
        xnseg = lnbuf.rearrange("p (c t) -> p c t", t=512)
        sbufs = [arC[:, 0:512], arC[:, 512:1024], arC[:, 1024:1536], arC[:, 1536:2048]]
        wgr = [arC[:, 4096 + k * 2048: 4096 + (k + 1) * 2048] for k in range(3)]
        wur = [arC[:, 10240 + k * 2048: 10240 + (k + 1) * 2048] for k in range(3)]
        wdr = qhb.rearrange("p (k j m) -> p k j m", k=4, m=P)

        xn_free = None
        gb_free = [None, None]
        wg_free = [None, None, None]
        wu_free = [None, None, None]
        wd_free = [None, None, None, None]
        xf3_free = [None, None, None, None]
        s_free = [None, None, None, None]
        t2_free = [None, None]
        ob_free = [None, None]
        psGU_free = [None, None, None, None]
        psD_free = [None, None]
        xft = xf.rearrange("p a b d -> p (a b d)")
        xqb = [xft[:, i * 512:(i + 1) * 512] for i in range(4)]

        for seg in range(NSEG_Q):
            txn = sc.do("sync",
                        lambda e, seg=seg:
                        e.dma_start(out=xnseg[:, :, :],
                                    in_=xnqscr[:, :, seg * 512:(seg + 1) * 512].rearrange("c p t -> p c t")),
                        [xn_free] + (BAR2B if seg == 0 else []), "dS")
            fuse_toks = []
            for j in range(NJ):
                wsl = j % 3
                twg = sc.do("sync",
                            lambda e, j=j, wsl=wsl: e.dma_start(out=wgr[wsl][:, :], in_=wgbf[j]),
                            [wg_free[wsl], conv_tok[("g", j // 16)]] +
                            (BAR2B if seg == 0 and j < 3 else []), "dS")
                twu = sc.do("sync",
                            lambda e, j=j, wsl=wsl: e.dma_start(out=wur[wsl][:, :], in_=wubf[j]),
                            [wu_free[wsl], conv_tok[("u", j // 16)]], "dS")
                bg = j % 2
                bankG = ps[bg]
                bankU = ps[2 + bg]

                def gmm(e, wsl=wsl, bankG=bankG):
                    w = wgr[wsl].rearrange("p (c m) -> p c m", m=P)
                    for c in range(NCD):
                        i_ = e.matmul(bankG, w[:, c, :], xnseg[:, c, :],
                                      start=(c == 0), stop=(c == NCD - 1))
                    return i_
                tgm = sc.do("tensor", gmm, (twg, txn, psGU_free[bg]), "sT")
                tga = sc.do("scalar",
                            lambda e, j=j, bg=bg, bankG=bankG:
                            e.activation(out=gbuf[:, bg, :], in_=bankG, func=AF.Silu,
                                         scale=sgc[:, j:j + 1]),
                            (tgm, gb_free[bg]), "sA")
                psGU_free[bg] = tga
                wg_free[wsl] = tgm

                def umm(e, wsl=wsl, bankU=bankU):
                    w = wur[wsl].rearrange("p (c m) -> p c m", m=P)
                    for c in range(NCD):
                        i_ = e.matmul(bankU, w[:, c, :], xnseg[:, c, :],
                                      start=(c == 0), stop=(c == NCD - 1))
                    return i_
                tum = sc.do("tensor", umm, (twu, txn, psGU_free[2 + bg]), "sT")
                wu_free[wsl] = tum
                tfu = sc.do("vector",
                            lambda e, j=j, bg=bg, bankU=bankU:
                            e.scalar_tensor_tensor(out=fuse[:, j, :], in0=bankU,
                                                   scalar=suc[:, j:j + 1], in1=gbuf[:, bg, :],
                                                   op0=ALU.mult, op1=ALU.mult),
                            (tum, tga, BAR2B if seg == 0 and j == 0 else None), "sV")
                psGU_free[2 + bg] = tfu
                gb_free[bg] = tfu
                fuse_toks.append(tfu)
            xn_free = fuse_toks[-1]

            for ob in range(NCD):
                dsl = ob % 2
                bankD = ps[4 + dsl]
                jg_mms = None
                for jg in range(8):
                    wdsl = jg % 4
                    twd = sc.do("sync",
                                lambda e, ob=ob, jg=jg, wdsl=wdsl:
                                e.dma_start(out=wdr[:, wdsl, :, :], in_=wdbf[ob, jg]),
                                [wd_free[wdsl], conv_tok[("d", ob // 4)]], "dS")

                    def dmm(e, jg=jg, wdsl=wdsl, bankD=bankD):
                        for jj in range(8):
                            i_ = e.matmul(bankD,
                                          wdr[:, wdsl, jj, :],
                                          fuse[:, jg * 8 + jj, :],
                                          start=(jg == 0 and jj == 0),
                                          stop=(jg == 7 and jj == 7))
                        return i_
                    deps = [twd] + (fuse_toks + [psD_free[dsl]] if jg == 0 else [])
                    tdm = sc.do("tensor", dmm, deps, "sT")
                    wd_free[wdsl] = tdm
                    jg_mms = tdm
                ksl = ob % 2
                tg_ = sc.do("scalar",
                            lambda e, ob=ob, ksl=ksl, bankD=bankD:
                            e.activation(out=gbuf[:, ksl, :], in_=bankD, func=AF.Silu,
                                         scale=sdc[:, ob:ob + 1]),
                            (jg_mms, gb_free[ksl]), "sA")
                psD_free[dsl] = tg_
                ssl4 = ob % 4
                tsl = sc.do("sync",
                            lambda e, seg=seg, ob=ob, ssl4=ssl4:
                            e.dma_start(out=sbufs[ssl4][:, :],
                                        in_=sscr[ob, :, seg * 512:(seg + 1) * 512]),
                            [s_free[ssl4], s_toks.get((ob, seg))], "dS")
                txq = sc.do("sync",
                            lambda e, seg=seg, ob=ob, ssl4=ssl4:
                            e.dma_start(out=xqb[ssl4][:, :],
                                        in_=dxqt[ob, :, seg * 512:(seg + 1) * 512]),
                            [xf3_free[ssl4]] + (BAR2B if seg == 0 and ob < 4 else []), "dS")
                t2 = sc.hz("vector",
                           lambda e, ksl=ksl, ssl4=ssl4:
                           e.tensor_mul(tmpg[:, ksl, :], gbuf[:, ksl, :], sbufs[ssl4][:, :]),
                           (tg_, t2_free[ksl], tsl))
                gb_free[ksl] = t2
                s_free[ssl4] = t2
                tout = sc.hz("vector",
                             lambda e, ksl=ksl, ssl4=ssl4:
                             e.tensor_add(outb[:, ksl, :], tmpg[:, ksl, :], xqb[ssl4][:, :]),
                             (t2, txq, ob_free[ksl]))
                t2_free[ksl] = tout
                xf3_free[ssl4] = tout
                tst = sc.do("scalar",
                            lambda e, seg=seg, ob=ob, ksl=ksl:
                            e.dma_start(out=dout[ob, :, seg * 512:(seg + 1) * 512],
                                        in_=outb[:, ksl, :]),
                            (tout,), "dA")
                ob_free[ksl] = tst

        FIN = dict(sc.cnt)
        sc.ops["sync"].append(([(s, FIN[s]) for s in DMA_SEM_NAMES if FIN[s] > 0],
                               None, None, 0))
        sc.check()

        def replay(name, e):
            for waits, fn, sig, inc in sc.ops[name]:
                for s, v in waits:
                    e.wait_ge(SEM[s], v)
                if fn is None:
                    continue
                inst = fn(e)
                if sig is not None:
                    inst.then_inc(SEM[sig], inc)

        with nc.Block() as block:
            @block.sync
            def _(e):
                replay("sync", e)

            @block.gpsimd
            def _(e):
                replay("gpsimd", e)

            @block.scalar
            def _(e):
                replay("scalar", e)

            @block.vector
            def _(e):
                replay("vector", e)

            @block.tensor
            def _(e):
                replay("tensor", e)

    return nc


_NC_CACHE = None


def _get_nc():
    global _NC_CACHE
    if _NC_CACHE is None:
        _NC_CACHE = build_program()
    return _NC_CACHE


def kernel(L0_attn_x, L0_attn_Wq, L0_attn_bq, L0_attn_Wk, L0_attn_bk,
           L0_attn_Wv, L0_attn_bv, L0_attn_Wo, L0_attn_bo,
           L0_ffn_w_gate, L0_ffn_w_up, L0_ffn_w_down,
           L0_ffn_s_gate, L0_ffn_s_up, L0_ffn_s_down):
    x = np.ascontiguousarray(np.asarray(L0_attn_x, dtype=np.float32))
    B = x.shape[0]
    asf = lambda a: np.ascontiguousarray(np.asarray(a, dtype=np.float32))
    wq = asf(L0_attn_Wq).reshape(NCD, P, D)
    wk = asf(L0_attn_Wk).reshape(NCD, P, D)
    wv = asf(L0_attn_Wv).reshape(NCD, P, D)
    wo = asf(L0_attn_Wo).reshape(NCD, P, D)
    bq = asf(L0_attn_bq).reshape(NCD, P)
    bk = asf(L0_attn_bk).reshape(NCD, P)
    bv = asf(L0_attn_bv).reshape(NCD, P)
    bo = asf(L0_attn_bo).reshape(NCD, P)
    wg8 = np.ascontiguousarray(
        np.asarray(L0_ffn_w_gate, dtype=np.int8).reshape(NCD, P, NJ, P).transpose(2, 1, 0, 3))
    wu8 = np.ascontiguousarray(
        np.asarray(L0_ffn_w_up, dtype=np.int8).reshape(NCD, P, NJ, P).transpose(2, 1, 0, 3))
    wd8 = np.ascontiguousarray(
        np.asarray(L0_ffn_w_down, dtype=np.int8)
        .reshape(8, 8, P, NCD, P).transpose(3, 0, 2, 1, 4).reshape(NCD, 8, P, 1024))
    sg = asf(L0_ffn_s_gate).reshape(NJ, P)
    su = asf(L0_ffn_s_up).reshape(NJ, P)
    sd = asf(L0_ffn_s_down).reshape(NCD, P)

    jj = np.arange(512)[None, :]
    pp = np.arange(P)[:, None]
    qmasks = []
    qmts = []
    for h in range(2):
        m = np.zeros((P, NQI * 512), dtype=np.float32)
        for qi in range(NQI):
            r = (2 * qi + h) % 4
            m[:, qi * 512:(qi + 1) * 512] = np.where(jj <= (pp + 128 * r), 0.0, -30000.0)
        qmasks.append(m.astype(ml_dtypes.bfloat16))
        kk = np.arange(P)[:, None]
        qq = np.arange(P)[None, :]
        mt = np.zeros((P, 256), dtype=np.float32)
        for s_off in range(2):
            vis = kk <= ((h - s_off) * 128 + qq)
            mt[:, s_off * 128:(s_off + 1) * 128] = np.where(vis, 0.0, -30000.0)
        qmts.append(mt.astype(ml_dtypes.bfloat16))

    nc = _get_nc()
    in_maps = []
    for core in range(8):
        b, h = divmod(core, 2)
        xb = np.ascontiguousarray(x[b])
        xqb = np.ascontiguousarray(x[b].reshape(32, P, D)[h::2].reshape(NQ, D))
        xqtb = np.ascontiguousarray(xqb.T.reshape(NCD, P, NQ))
        in_maps.append({
            "x": xb, "xq": xqb, "xqt": xqtb, "qmask": qmasks[h], "qmt": qmts[h],
            "wq": wq, "wk": wk, "wv": wv, "wo": wo,
            "bq": bq, "bk": bk, "bv": bv, "bo": bo,
            "wg8": wg8, "wu8": wu8, "wd8": wd8,
            "sg": sg, "su": su, "sd": sd,
        })
    res = run_bass_kernel_spmd(nc, in_maps, core_ids=list(range(8)))
    y = np.empty((B, S, D), dtype=np.float32)
    for core in range(8):
        b, h = divmod(core, 2)
        o = np.asarray(res.results[core]["out"], dtype=np.float32)
        o = o.reshape(D, NQ).T.reshape(NQI, P, D)
        y[b].reshape(32, P, D)[h::2] = o
    return y


if __name__ == "__main__":
    nc = build_program()
    n = sum(1 for _ in nc.all_instructions())
    print("instructions:", n)
